# version 27
# baseline (speedup 1.0000x reference)
import math

import numpy as np
import ml_dtypes

import concourse.bass as bass
import concourse.mybir as mybir

F32 = mybir.dt.float32
BF16 = mybir.dt.bfloat16
AF = mybir.ActivationFunctionType
ALU = mybir.AluOpType
AX = mybir.AxisListType

BF = ml_dtypes.bfloat16


class Par:
    def __init__(self, T=2048, D=2048, E=8192, B=4, NG=2, SUPER=8):
        self.T, self.D, self.E, self.B = T, D, E, B
        self.H = D // 128
        self.TT = T // 128
        self.QT = self.TT // 2
        self.TQ = T // 2
        self.DC = D // 128
        self.EC = E // 128
        self.NG = NG
        assert self.H % NG == 0 and NG * 128 <= 512
        self.NGRP = self.H // NG
        self.SUPER = min(SUPER, self.EC)
        assert self.EC % self.SUPER == 0
        self.NSUP = self.EC // self.SUPER
        self.EPS = 1e-5
        self.QEPS = 1e-8
        self.ISQT = 1.0 / math.sqrt(T)
        self.owned = {0: list(range(self.TT - 1, 0, -2)),
                      1: list(range(self.TT - 2, -1, -2))}
        self.n_c = [len([g for g in self.owned[0] if g >= c]) for c in range(self.TT)]

    def prefix(self, parity, c):
        own = self.owned[parity]
        lst = [g for g in own if g >= c]
        if len(lst) < self.n_c[c]:
            lst = lst + [max(g for g in own if g < c)]
        assert len(lst) == self.n_c[c]
        return lst


def core_consts(par: Par, parity: int):
    TT, TQ = par.TT, par.TQ
    own = par.owned[parity]
    gmax = {}
    for g in own:
        gm = g
        if g + 1 < TT and par.n_c[g + 1] > 0 and par.prefix(parity, g + 1)[-1] == g:
            gm = g + 1
        gmax[g] = gm
    masks = np.ones((TT, 128, 128), np.float32)
    triu = (np.arange(128)[:, None] <= np.arange(128)[None, :]).astype(np.float32)
    for c in range(TT):
        if par.n_c[c] == 0:
            continue
        last = par.prefix(parity, c)[-1]
        if last == c:
            masks[c] = triu
        elif last < c:
            masks[c] = 0.0
    G = np.zeros((TT, TQ), np.float32)
    counts = np.zeros((1, TQ), np.float32)
    for k, g in enumerate(own):
        G[gmax[g], k * 128:(k + 1) * 128] = 1.0
        counts[0, k * 128:(k + 1) * 128] = 128.0 * (TT - 1 - gmax[g])
    U = np.zeros((TT, 128, TT), np.float32)
    for c in range(TT):
        U[c, :, :c] = 1.0
    return dict(masks=masks.astype(BF), Gmat=G.astype(BF),
                counts=counts.astype(BF), Umat=U.astype(BF),
                ident=np.eye(128, dtype=np.float32))


def host_layouts(par, Wq, Wk, Wv, Wo, Wup, Wgate, Wdown):
    DC, EC, D = par.DC, par.EC, par.D
    def colmajor(Wm):
        a = Wm.reshape(DC, 128, DC, 128)
        return np.ascontiguousarray(a.transpose(2, 1, 0, 3)).reshape(DC, 128, D)
    def up_tiles(Wm):
        a = Wm.reshape(EC, 128, DC, 128)
        return np.ascontiguousarray(a.transpose(0, 3, 2, 1)).reshape(EC, 128, D)
    return (colmajor(Wq), colmajor(Wk),
            np.ascontiguousarray(Wv.reshape(DC, 128, D)),
            np.ascontiguousarray(Wo.reshape(DC, 128, D)),
            up_tiles(Wup), up_tiles(Wgate),
            np.ascontiguousarray(np.ascontiguousarray(Wdown.T).reshape(EC, 128, D)))


class Tok:
    __slots__ = ("sem", "val")
    def __init__(self, sem, val):
        self.sem, self.val = sem, val


class Counter:
    def __init__(self, nc, name, step=1):
        self.sem = nc.alloc_semaphore(name)
        self.cnt = 0
        self.step = step

    def inc(self, instr):
        instr.then_inc(self.sem, self.step)
        self.cnt += self.step
        return Tok(self.sem, self.cnt)


class Waiter:
    def __init__(self):
        self.eng = None
        self.hw = {}

    def wait(self, *toks):
        for t in toks:
            if t is None:
                continue
            k = t.sem.num
            if self.hw.get(k, 0) < t.val:
                self.eng.wait_ge(t.sem, t.val)
                self.hw[k] = t.val


class Buf:
    __slots__ = ("ap", "w", "reads", "others")
    def __init__(self, ap):
        self.ap = ap
        self.w = {}
        self.reads = {}
        self.others = []

    def _merge(self, d, tok):
        cur = d.get(tok.sem.num)
        if cur is None or cur.val < tok.val:
            d[tok.sem.num] = tok

    def rd(self, waiter):
        for b in (self,) + tuple(self.others):
            waiter.wait(*b.w.values())

    def wr(self, waiter):
        for b in (self,) + tuple(self.others):
            waiter.wait(*b.w.values())
            waiter.wait(*b.reads.values())

    def note_read(self, tok):
        self._merge(self.reads, tok)

    def note_write(self, tok):
        self._merge(self.w, tok)
        self.reads = {}


def link(*bufs):
    for b in bufs:
        b.others = [o for o in bufs if o is not b]


def link_pair(a, b):
    if b not in a.others:
        a.others.append(b)
    if a not in b.others:
        b.others.append(a)


def build_program(p: Par, debug=False):
    nc = bass.Bass("TRN2", target_bir_lowering=False)

    xs = nc.dram_tensor("xs", [p.T, p.D], F32, kind="ExternalInput").ap()
    xq = nc.dram_tensor("xq", [p.TQ, p.D], F32, kind="ExternalInput").ap()
    WqL = nc.dram_tensor("WqL", [p.DC, 128, p.D], F32, kind="ExternalInput").ap()
    WkL = nc.dram_tensor("WkL", [p.DC, 128, p.D], F32, kind="ExternalInput").ap()
    WvR = nc.dram_tensor("WvR", [p.DC, 128, p.D], F32, kind="ExternalInput").ap()
    WoR = nc.dram_tensor("WoR", [p.DC, 128, p.D], F32, kind="ExternalInput").ap()
    UpL = nc.dram_tensor("UpL", [p.EC, 128, p.D], F32, kind="ExternalInput").ap()
    GateL = nc.dram_tensor("GateL", [p.EC, 128, p.D], F32, kind="ExternalInput").ap()
    DownR = nc.dram_tensor("DownR", [p.EC, 128, p.D], F32, kind="ExternalInput").ap()
    masks_in = nc.dram_tensor("masks", [p.TT, 128, 128], BF16, kind="ExternalInput").ap()
    G_in = nc.dram_tensor("Gmat", [p.TT, p.TQ], BF16, kind="ExternalInput").ap()
    counts_in = nc.dram_tensor("counts", [1, p.TQ], BF16, kind="ExternalInput").ap()
    U_in = nc.dram_tensor("Umat", [p.TT, 128, p.TT], BF16, kind="ExternalInput").ap()
    ident_in = nc.dram_tensor("ident", [128, 128], F32, kind="ExternalInput").ap()
    out_d = nc.dram_tensor("out", [p.TQ, p.D], F32, kind="ExternalOutput").ap()
    if debug:
        dbg_scales = nc.dram_tensor("dbg_scales", [128, 16], F32, kind="ExternalOutput").ap()
        dbg_ternup = nc.dram_tensor("dbg_ternup", [128, p.D], F32, kind="ExternalOutput").ap()
        dbg_terngate = nc.dram_tensor("dbg_terngate", [128, p.D], F32, kind="ExternalOutput").ap()
        dbg_terndown = nc.dram_tensor("dbg_terndown", [128, p.SUPER * p.D], F32, kind="ExternalOutput").ap()
        dbg_hidden = nc.dram_tensor("dbg_hidden", [128, p.SUPER * p.TQ], F32, kind="ExternalOutput").ap()
        dbg_down = nc.dram_tensor("dbg_down", [128, (p.TQ // 128) * p.D], F32, kind="ExternalOutput").ap()
        dbg_h2t = nc.dram_tensor("dbg_h2t", [128, p.DC * p.TQ], F32, kind="ExternalOutput").ap()
        dbg_oh = nc.dram_tensor("dbg_oh", [128, p.H * p.TQ], F32, kind="ExternalOutput").ap()
        dbg_q = nc.dram_tensor("dbg_q", [128, p.H * p.TQ], F32, kind="ExternalOutput").ap()
        dbg_k = nc.dram_tensor("dbg_k", [128, p.NG * p.T], F32, kind="ExternalOutput").ap()
        dbg_v = nc.dram_tensor("dbg_v", [128, p.TT * p.NG * 128], F32, kind="ExternalOutput").ap()
        dbg_den = nc.dram_tensor("dbg_den", [1, p.TQ], F32, kind="ExternalOutput").ap()
        dbg_ao = nc.dram_tensor("dbg_ao", [128, p.QT * p.D], F32, kind="ExternalOutput").ap()
    x1_spill = nc.dram_tensor("x1_spill", [p.TQ, p.D], F32, kind="Internal").ap()
    den_spill = nc.dram_tensor("den_spill", [p.H, p.TQ], F32, kind="Internal").ap()

    def alias_at(handle, name, shape, dt, delta=0):
        addr = nc.lookup_mloc(handle).addr + delta
        return Buf(nc.alloc_sbuf_tensor_at(name, list(shape), dt, offset=addr).ap())

    gw = p.NG * 128
    a_cols = p.DC * p.T
    arena_a_h = nc.alloc_sbuf_tensor("arena_a", [128, a_cols], BF16)
    h_T = Buf(arena_a_h.ap())
    attn_out_h = nc.alloc_sbuf_tensor_at(
        "attn_out", [128, p.QT * p.D], F32,
        offset=nc.lookup_mloc(arena_a_h).addr)
    down_acc = alias_at(arena_a_h, "down_acc", (128, (p.TQ // 128) * p.D), F32)
    attn_outQ = [Buf(attn_out_h.ap()[:, qk * p.D:(qk + 1) * p.D])
                 for qk in range(p.QT)]
    for _aq in attn_outQ:
        link_pair(_aq, h_T)
        link_pair(_aq, down_acc)
    link_pair(h_T, down_acc)

    b_cols = max(p.DC * p.TQ, p.SUPER * p.D)
    arena_b = nc.alloc_sbuf_tensor("arena_b", [128, b_cols], BF16).ap()
    hq_T = Buf(arena_b)
    ternDown = Buf(arena_b)
    link(hq_T, ternDown)
    outheadsH = [Buf(arena_b[:, h * p.TQ:(h + 1) * p.TQ]) for h in range(p.H)]
    for _oh in outheadsH:
        link_pair(_oh, hq_T)
        link_pair(_oh, ternDown)

    arena_c = nc.alloc_sbuf_tensor("arena_c", [128, p.H * p.TQ], BF16).ap()
    Qb = Buf(arena_c)
    h2_T = Buf(arena_c)
    link(Qb, h2_T)

    o1, o2, o3 = p.NG * p.T, p.NG * p.T + p.TT * gw, p.NG * p.T + p.TT * gw + p.DC * gw
    d_attn = o3 + 2 * p.TQ + p.TT * 128
    d_ffn = p.SUPER * p.TQ + 4 * p.D
    arena_d_h = nc.alloc_sbuf_tensor("arena_d", [128, max(d_attn, d_ffn, 8192)], BF16)
    arena_d = arena_d_h.ap()
    Kg = Buf(arena_d[:, 0:o1])
    Vg = Buf(arena_d[:, o1:o2])
    wvg = Buf(arena_d[:, o2:o3])
    bc_sb = alias_at(arena_d_h, "bc_sb", (128, p.TQ), F32, delta=2 * o3)
    den_f = alias_at(arena_d_h, "den_f", (1, p.TQ), F32, delta=2 * o3)
    masks_b = Buf(arena_d[:, o3 + 2 * p.TQ:o3 + 2 * p.TQ + p.TT * 128])
    hidden = Buf(arena_d[:, 0:p.SUPER * p.TQ])
    tUpA = Buf(arena_d[:, p.SUPER * p.TQ + 0 * p.D:p.SUPER * p.TQ + 1 * p.D])
    tUpB = Buf(arena_d[:, p.SUPER * p.TQ + 1 * p.D:p.SUPER * p.TQ + 2 * p.D])
    tGtA = Buf(arena_d[:, p.SUPER * p.TQ + 2 * p.D:p.SUPER * p.TQ + 3 * p.D])
    tGtB = Buf(arena_d[:, p.SUPER * p.TQ + 3 * p.D:p.SUPER * p.TQ + 4 * p.D])
    link_pair(bc_sb, den_f)
    _s0 = p.SUPER * p.TQ
    _attn_rng = [(Kg, 0, o1), (Vg, o1, o2), (wvg, o2, o3),
                 (bc_sb, o3, o3 + 2 * p.TQ), (den_f, o3, o3 + 2 * p.TQ),
                 (masks_b, o3 + 2 * p.TQ, o3 + 2 * p.TQ + p.TT * 128)]
    _ffn_rng = [(hidden, 0, _s0),
                (tUpA, _s0, _s0 + p.D), (tUpB, _s0 + p.D, _s0 + 2 * p.D),
                (tGtA, _s0 + 2 * p.D, _s0 + 3 * p.D),
                (tGtB, _s0 + 3 * p.D, _s0 + 4 * p.D)]
    for _a, _a0, _a1 in _attn_rng:
        for _f, _f0, _f1 in _ffn_rng:
            if _a0 < _f1 and _f0 < _a1:
                link_pair(_a, _f)

    xf_cols = 3 * p.D
    arena_x_h = nc.alloc_sbuf_tensor("arena_x", [128, xf_cols], F32)
    arena_x = arena_x_h.ap()
    xtA = Buf(arena_x[:, 0:p.D])
    xtB = Buf(arena_x[:, p.D:2 * p.D])
    fsUpA = Buf(arena_x[:, 0:p.D // 2])
    fsUpB = Buf(arena_x[:, p.D // 2:p.D])
    fsGtA = Buf(arena_x[:, p.D:p.D + p.D // 2])
    fsGtB = Buf(arena_x[:, p.D + p.D // 2:2 * p.D])
    fsDnA = Buf(arena_x[:, 2 * p.D:2 * p.D + p.D // 2])
    fsDnB = Buf(arena_x[:, 2 * p.D + p.D // 2:3 * p.D])
    wqkA = alias_at(arena_x_h, "wqkA", (128, p.D // 2), BF16, delta=8 * p.D)
    wqkB = alias_at(arena_x_h, "wqkB", (128, p.D // 2), BF16, delta=9 * p.D)
    hbufA = alias_at(arena_x_h, "hbufA", (128, p.D), BF16, delta=8 * p.D)
    hbufB = alias_at(arena_x_h, "hbufB", (128, p.D), BF16, delta=10 * p.D)
    link_pair(xtA, fsUpA)
    link_pair(xtA, fsUpB)
    link_pair(xtB, fsGtA)
    link_pair(xtB, fsGtB)
    wqkC = alias_at(arena_x_h, "wqkC", (128, p.D // 2), BF16, delta=10 * p.D)
    wqkD = alias_at(arena_x_h, "wqkD", (128, p.D // 2), BF16, delta=11 * p.D)
    xtC = alias_at(arena_x_h, "xtC", (128, p.D), F32, delta=8 * p.D)
    for _b in (fsDnA, fsDnB, hbufA, hbufB, wqkA, wqkB, wqkC, wqkD):
        link_pair(xtC, _b)
    link_pair(hbufA, wqkA)
    link_pair(hbufA, wqkB)
    link_pair(hbufA, fsDnA)
    link_pair(wqkA, fsDnA)
    link_pair(wqkB, fsDnA)
    link_pair(hbufB, fsDnB)
    link_pair(hbufB, wqkC)
    link_pair(hbufB, wqkD)
    link_pair(fsDnB, wqkC)
    link_pair(fsDnB, wqkD)

    def sb(name, shape, dt=F32):
        return Buf(nc.alloc_sbuf_tensor(name, list(shape), dt).ap())

    stripA = sb("stripA", (128, max(p.QT * 128, p.D // 2)), BF16)
    stripB = sb("stripB", (128, max(p.QT * 128, p.D // 2)), BF16)
    qmA, qmB = stripA, stripB
    _sw = max(p.QT * 128, p.D // 2)
    stQ = [Buf((stripA if i < 2 else stripB).ap[:, (i % 2) * (_sw // 2):
                                                 (i % 2 + 1) * (_sw // 2)])
           for i in range(4)]
    for i, sq in enumerate(stQ):
        link_pair(sq, stripA if i < 2 else stripB)
    silu_t = sb("silu_t", (128, p.TQ), BF16)
    den_fA = sb("den_fA", (1, p.TQ))
    den_fB = sb("den_fB", (1, p.TQ))
    counts_b = sb("counts_b", (1, p.TQ), BF16)
    G_b_h = nc.alloc_sbuf_tensor("G_b", [p.TT, p.TQ], BF16)
    G_b = Buf(G_b_h.ap())
    qmC = alias_at(G_b_h, "qmC", (128, min(p.TQ, p.D // 2)), BF16)
    link(G_b, qmC)
    U_b = sb("U_b", (128, p.TT * p.TT), BF16)
    ident_b = sb("ident_b", (128, 128), F32)
    ones128 = sb("ones128", (128, 128), BF16)
    onesrow = sb("onesrow", (1, 128), F32)
    onescol = sb("onescol", (128, 1), F32)
    vsufT = sb("vsufT", (p.TT, gw), BF16)
    absP = sb("absP", (128, 3 * 72), F32)
    stats = sb("stats", (128, 6 * max(1, p.D // 512)), F32)
    mvA = sb("mvA", (128, 2), F32)
    mvC = sb("mvC", (128, 2), F32)
    nmrC = sb("nmrC", (128, 1), F32)
    rstdC = sb("rstdC", (128, 1), F32)
    mvB = sb("mvB", (128, 2), F32)
    nmrA = sb("nmrA", (128, 1), F32)
    nmrB = sb("nmrB", (128, 1), F32)
    rstdA = sb("rstdA", (128, 1), F32)
    rstdB = sb("rstdB", (128, 1), F32)
    sc_misc = sb("sc_misc", (1, 8), F32)
    s_up = sb("s_up", (128, 8), F32)
    s_gate = sb("s_gate", (128, 8), F32)
    s_down = sb("s_down", (128, 8), F32)
    epsd = sb("epsd", (128, 1), F32)
    epsb = sb("epsb", (128, 1), F32)
    epsp = sb("epsp", (128, 1), F32)
    issq = sb("issq", (128, 1), F32)

    R01 = nc.alloc_psum_tensor("R01", [128, 2048], F32).ap()
    R23 = nc.alloc_psum_tensor("R23", [128, 2048], F32).ap()
    psSA = Buf(R01[:, 0:1024])
    psSB = Buf(R01[:, 1024:2048])
    psOH = Buf(R23[:, 0:1024])
    psDen = Buf(R23[:, 1024:2048])
    psTr = Buf(R01[:, 0:512])
    psTr2 = Buf(R23[:, 0:512])
    downA = Buf(R01)
    downB = Buf(R23)
    psSA.others = [psTr, downA]
    psSB.others = [downA]
    psTr.others = [psSA, downA]
    downA.others = [psSA, psSB, psTr]
    psOH.others = [psTr2, downB]
    psDen.others = [downB]
    psTr2.others = [psOH, downB]
    downB.others = [psOH, psDen, psTr2]
    qkv8 = [Buf(R01[:, i * 512:(i + 1) * 512]) for i in range(4)] + \
           [Buf(R23[:, i * 512:(i + 1) * 512]) for i in range(4)]
    for i, hb in enumerate(qkv8):
        if i < 4:
            over = [psSA if i < 2 else psSB, downA] + ([psTr] if i == 0 else [])
        else:
            over = [psOH if i < 6 else psDen, downB] + ([psTr2] if i == 4 else [])
        hb.others = over
        for o in over:
            o.others = o.others + [hb]
    sSq = [Buf(R01[:, i * 512:(i + 1) * 512]) for i in range(4)]
    for i, sq in enumerate(sSq):
        over = [psSA if i < 2 else psSB, downA] + ([psTr] if i == 0 else [])
        over += [qkv8[i]]
        for o in over:
            link_pair(sq, o)
    qkv_rr = [0]

    def next_qreg():
        r = qkv8[qkv_rr[0] % 8]
        qkv_rr[0] += 1
        return r

    eP = Counter(nc, "eP")
    eV = Counter(nc, "eV")
    eA = Counter(nc, "eA")
    eO = Counter(nc, "eO")
    dma_ctrs = {}

    def qof(buf, tag):
        key = (id(buf), tag)
        if key not in dma_ctrs:
            dma_ctrs[key] = Counter(nc, f"q{tag}.{len(dma_ctrs)}", 16)
        return dma_ctrs[key]

    W = {k: Waiter() for k in ("sync", "gpsimd", "vector", "scalar", "tensor")}

    class _FakeIns:
        __slots__ = ("incs",)
        def __init__(self, incs):
            self.incs = incs
        def then_inc(self, sem, step):
            self.incs.append((sem, step))
            return self

    class Rec:
        def __init__(self):
            self.calls = []
        def wait_ge(self, sem, val):
            self.calls.append(("w", sem, val))
        def __getattr__(self, m):
            def f(*a, **k):
                incs = []
                self.calls.append(("c", m, a, k, incs))
                return _FakeIns(incs)
            return f

    recs = {k: Rec() for k in W}
    for k in W:
        W[k].eng = recs[k]

    with nc.Block() as blk:

        def sec(engname):
            def runner(fn):
                fn(recs[engname], W[engname])
            return runner

        def dma_in(engname, dst, dst_ap, src_ap, deps=(), tag="i"):
            q = qof(dst, tag)
            @sec(engname)
            def _(e, w):
                w.wait(*deps)
                dst.wr(w)
                tok = q.inc(e.dma_start(out=dst_ap, in_=src_ap))
                dst.note_write(tok)

        def dma_store(engname, dst_ap, src, src_ap, tag="o"):
            q = qof(src, tag)
            @sec(engname)
            def _(e, w):
                src.rd(w)
                tok = q.inc(e.dma_start(out=dst_ap, in_=src_ap))
                src.note_read(tok)

        dma_in("gpsimd", masks_b,
               masks_b.ap.rearrange("p (c n) -> p c n", c=p.TT),
               masks_in.rearrange("c p n -> p c n"))
        dma_in("gpsimd", G_b, G_b.ap[:, :], G_in[:, :])
        dma_in("gpsimd", counts_b, counts_b.ap[:, :], counts_in[:, :])
        dma_in("gpsimd", U_b,
               U_b.ap.rearrange("p (c n) -> p c n", c=p.TT),
               U_in.rearrange("c p n -> p c n"))
        dma_in("gpsimd", ident_b, ident_b.ap[:, :], ident_in[:, :])

        @sec("vector")
        def _(v, w):
            epsp.wr(w)
            D24 = float(1.5 * 2.0 ** -24)
            t = eV.inc(v.memset(epsp.ap[:, :], 1.0))
            for add in (D24, -1.0, -D24):
                w.wait(t)
                t = eV.inc(v.tensor_single_scalar(epsp.ap[:, :], epsp.ap[:, :],
                                                  add, ALU.add))
            epsp.note_write(t)
            for buf, val in ((ones128, 1.0), (onesrow, 1.0), (onescol, 1.0),
                             (epsb, 1e-5), (issq, float(p.T))):
                buf.wr(w)
                buf.note_write(eV.inc(v.memset(buf.ap[:, :], val)))
            w.wait(epsb.w[eV.sem.num])
            t = eV.inc(v.tensor_single_scalar(epsb.ap[:, :], epsb.ap[:, :],
                                              2.526212491824791e-13, ALU.add))
            epsb.note_write(t)

        @sec("scalar")
        def _(sc, w):
            issq.rd(w)
            t1 = eA.inc(sc.activation(issq.ap[:, :], issq.ap[:, :], AF.Ln))
            w.wait(t1)
            t2 = eA.inc(sc.activation(issq.ap[:, :], issq.ap[:, :], AF.Exp,
                                      scale=-0.5))
            issq.note_write(t2)

        def emit_ln_stats(src_buf, src_ap_fn, mv, rstd, eps_bias):
            ngp = max(1, p.D // 512)
            wdt = min(512, p.D)

            @sec("vector")
            def _(v, w):
                src_buf.rd(w)
                stats.wr(w)
                tk = None
                for gx in range(ngp):
                    tk = eV.inc(v.bn_stats(stats.ap[:, gx * 6:gx * 6 + 6],
                                           src_ap_fn()[:, gx * wdt:(gx + 1) * wdt]))
                stats.note_write(tk)
                src_buf.note_read(tk)
                w.wait(tk)
                mv.wr(w)
                mv.note_write(eV.inc(v.bn_aggr(mv.ap[:, :], stats.ap[:, 0:6 * ngp])))
                stats.note_read(Tok(eV.sem, eV.cnt))

            @sec("scalar")
            def _(s, w):
                mv.rd(w)
                rstd.wr(w)
                if isinstance(eps_bias, Buf):
                    eps_bias.rd(w)
                    bias = eps_bias.ap[:, 0:1]
                else:
                    bias = eps_bias
                t1 = eA.inc(s.activation(rstd.ap[:, :], mv.ap[:, 1:2], AF.Ln, bias=bias))
                mv.note_read(t1)
                if isinstance(eps_bias, Buf):
                    eps_bias.note_read(t1)
                w.wait(t1)
                rstd.note_write(eA.inc(
                    s.activation(rstd.ap[:, :], rstd.ap[:, :], AF.Exp, scale=-0.5)))

        def emit_normalize(src_buf, src_ap_fn, mv, rstd, dst_buf, dst_ap):
            nmr = nmrA if rstd is rstdA else (nmrB if rstd is rstdB else nmrC)

            @sec("vector")
            def _(v, w):
                mv.rd(w)
                rstd.rd(w)
                nmr.wr(w)
                t0 = eV.inc(v.tensor_scalar(nmr.ap[:, :], mv.ap[:, 0:1],
                                            -1.0, rstd.ap[:, 0:1],
                                            ALU.mult, ALU.mult))
                nmr.note_write(t0)
                mv.note_read(t0)
                rstd.note_read(t0)

            @sec("scalar")
            def _(sc, w):
                src_buf.rd(w)
                rstd.rd(w)
                nmr.rd(w)
                dst_buf.wr(w)
                tk = eA.inc(sc.activation(dst_ap, src_ap_fn(), AF.Identity,
                                          bias=nmr.ap[:, 0:1],
                                          scale=rstd.ap[:, 0:1]))
                dst_buf.note_write(tk)
                src_buf.note_read(tk)
                rstd.note_read(tk)
                nmr.note_read(tk)

        def load_xtile(xt, src_ap):
            hD = p.D // 2
            dma_in("sync", xt, xt.ap[:, 0:hD], src_ap[:, 0:hD])
            dma_in("gpsimd", xt, xt.ap[:, hD:p.D], src_ap[:, hD:p.D], tag="i2")

        def emit_transposes(hbuf, dstT, dst_width, t0, evac="scalar"):
            for grp in range((p.DC + 3) // 4):
                nj = min(4, p.DC - grp * 4)
                trb = psTr if grp % 2 == 0 else psTr2

                @sec("tensor")
                def _(t, w, grp=grp, nj=nj, trb=trb):
                    hbuf.rd(w)
                    ident_b.rd(w)
                    trb.wr(w)
                    ins = None
                    for j in range(nj):
                        dc = grp * 4 + j
                        ins = t.transpose(trb.ap[:, j * 128:(j + 1) * 128],
                                          hbuf.ap[:, dc * 128:(dc + 1) * 128],
                                          ident_b.ap[:, :])
                    tk = eP.inc(ins)
                    trb.note_write(tk)
                    hbuf.note_read(tk)
                    ident_b.note_read(tk)

                if evac == "mixed" and grp % 2 == 0:
                    @sec("vector")
                    def _(v, w, grp=grp, nj=nj, trb=trb):
                        trb.rd(w)
                        dstT.wr(w)
                        dst = dstT.ap.rearrange("p (c t) -> p c t", c=p.DC)
                        tk = eV.inc(v.tensor_copy(
                            dst[:, grp * 4:grp * 4 + nj, t0:t0 + 128],
                            trb.ap.rearrange("p (j t) -> p j t", j=4)[:, 0:nj, :]))
                        trb.note_read(tk)
                        dstT.note_write(tk)
                else:
                    @sec("scalar")
                    def _(sc, w, grp=grp, nj=nj, trb=trb):
                        trb.rd(w)
                        dstT.wr(w)
                        dst = dstT.ap.rearrange("p (c t) -> p c t", c=p.DC)
                        tk = eA.inc(sc.activation(
                            dst[:, grp * 4:grp * 4 + nj, t0:t0 + 128],
                            trb.ap.rearrange("p (j t) -> p j t", j=4)[:, 0:nj, :],
                            AF.Identity))
                        trb.note_read(tk)
                        dstT.note_write(tk)

        xrot = [(xtA, mvA, rstdA), (xtB, mvB, rstdB), (xtC, mvC, rstdC)]

        def emit_ln1():
            tiles = [("xs", t) for t in range(p.TT)] + [("xq", k) for k in range(p.QT)]
            n = len(tiles)

            def ld(i):
                srcname, ti = tiles[i]
                xt = xrot[i % 3][0]
                src = xs if srcname == "xs" else xq
                load_xtile(xt, src[ti * 128:(ti + 1) * 128, :])

            def st(i):
                xt, mv, rstd = xrot[i % 3]
                emit_ln_stats(xt, lambda xt=xt: xt.ap, mv, rstd, epsb)

            ld(0)
            st(0)
            if n > 1:
                ld(1)
            for i, (srcname, ti) in enumerate(tiles):
                xt, mv, rstd = xrot[i % 3]
                dstT = h_T if srcname == "xs" else hq_T
                dstW = p.T if srcname == "xs" else p.TQ
                t0 = ti * 128
                emit_normalize(xt, lambda xt=xt: xt.ap, mv, rstd, xt, xt.ap[:, :])
                if i + 1 < n:
                    st(i + 1)
                if i + 2 < n:
                    ld(i + 2)
                emit_transposes(xt, dstT, dstW, t0, evac="mixed")

        emit_ln1()

        abs_state = {"next": 0, "next_act": 0, "next_dve": 0}
        abs_sched_act = [(2, ec) for ec in range(p.EC)]
        abs_sched_dve = [(wi, ec) for ec in range(p.EC) for wi in (0, 1)]
        abs_sched = abs_sched_act + abs_sched_dve

        def emit_abs(n, which="any"):
            for _ in range(n):
                if which in ("act", "any") and \
                        abs_state["next_act"] < len(abs_sched_act):
                    wi, ec = abs_sched_act[abs_state["next_act"]]
                    abs_state["next_act"] += 1
                elif abs_state["next_dve"] < len(abs_sched_dve):
                    wi, ec = abs_sched_dve[abs_state["next_dve"]]
                    abs_state["next_dve"] += 1
                elif abs_state["next_act"] < len(abs_sched_act):
                    wi, ec = abs_sched_act[abs_state["next_act"]]
                    abs_state["next_act"] += 1
                else:
                    return
                i = abs_state["next"]
                abs_state["next"] += 1
                src = (UpL, GateL, DownR)[wi]
                buf = xtA if i % 2 == 0 else xtB
                col = wi * 72 + ec
                dma_in("sync" if i % 2 == 0 else "gpsimd", buf,
                       buf.ap[:, :], src[ec], tag="i" if i % 2 == 0 else "i3")
                if wi < 2:
                    @sec("vector")
                    def _(v2, w, buf=buf, col=col):
                        buf.rd(w)
                        tk = eV.inc(v2.reduce_sum(absP.ap[:, col:col + 1],
                                                  buf.ap[:, :], axis=AX.X,
                                                  apply_absolute_value=True))
                        buf.note_read(tk)
                        absP.note_write(tk)
                else:
                    @sec("scalar")
                    def _(s2, w, buf=buf, col=col):
                        buf.rd(w)
                        tk = eA.inc(s2.activation(buf.ap[:, :], buf.ap[:, :], AF.Abs,
                                                  accum_out=absP.ap[:, col:col + 1]))
                        buf.note_read(tk)
                        buf.note_write(tk)
                        absP.note_write(tk)

        wpairs = [(wqkA, wqkB), (wqkC, wqkD)]

        def load_w_halves(dram, nb, pair, deps=()):
            dma_in("gpsimd", pair[0], pair[0].ap[:, :], dram[nb][:, 0:p.D // 2], deps)
            dma_in("gpsimd", pair[1], pair[1].ap[:, :], dram[nb][:, p.D // 2:p.D], deps)

        def emit_proj_mms(nb, dstT_buf, dst_off, rhsT_buf, rhs_w, pair,
                          scale_buf=None):
            wsA, wsB = pair
            nqb = (rhs_w + 511) // 512
            for tb in range(nqb):
                wdt = min(512, rhs_w - tb * 512)
                reg = next_qreg()

                @sec("tensor")
                def _(t, w, tb=tb, reg=reg, wdt=wdt):
                    wsA.rd(w)
                    wsB.rd(w)
                    rhsT_buf.rd(w)
                    reg.wr(w)
                    rT = rhsT_buf.ap.rearrange("p (c t) -> p c t", c=p.DC)
                    ins = None
                    for dc in range(p.DC):
                        wsl = (wsA if dc < p.DC // 2 else wsB)
                        wof = (dc if dc < p.DC // 2 else dc - p.DC // 2) * 128
                        ins = t.matmul(reg.ap[:, 0:wdt],
                                       wsl.ap[:, wof:wof + 128],
                                       rT[:, dc, tb * 512:tb * 512 + wdt],
                                       start=(dc == 0), stop=(dc == p.DC - 1))
                    tk = eP.inc(ins)
                    reg.note_write(tk)
                    wsA.note_read(tk)
                    wsB.note_read(tk)
                    rhsT_buf.note_read(tk)

                @sec("vector")
                def _(v, w, tb=tb, reg=reg, wdt=wdt):
                    reg.rd(w)
                    dstT_buf.wr(w)
                    dst = dstT_buf.ap[:, dst_off + tb * 512:dst_off + tb * 512 + wdt]
                    if scale_buf is None:
                        tk = eV.inc(v.tensor_copy(dst, reg.ap[:, 0:wdt]))
                    else:
                        scale_buf.rd(w)
                        tk = eV.inc(v.tensor_single_scalar(
                            dst, reg.ap[:, 0:wdt], scale_buf.ap[:, 0:1], ALU.mult))
                        scale_buf.note_read(tk)
                    reg.note_read(tk)
                    dstT_buf.note_write(tk)

        def kpair(ki):
            return wpairs[(p.DC + ki) % 2]

        nK = p.NGRP * p.NG
        load_w_halves(WqL, 0, wpairs[0])
        if p.DC > 1:
            load_w_halves(WqL, 1, wpairs[1])
        for nb in range(p.DC):
            emit_proj_mms(nb, Qb, nb * p.TQ, hq_T, p.TQ, wpairs[nb % 2],
                          scale_buf=issq)
            if nb + 2 < p.DC:
                load_w_halves(WqL, nb + 2, wpairs[nb % 2])
            elif nb + 2 - p.DC < min(2, nK):
                load_w_halves(WkL, nb + 2 - p.DC, kpair(nb + 2 - p.DC))
            emit_abs(2)

        def load_wvg(gi):
            h0 = gi * p.NG
            qwv = qof(wvg, "i")

            @sec("gpsimd")
            def _(g, w):
                wvg.wr(w)
                tk = None
                for dc in range(p.DC):
                    tk = qwv.inc(g.dma_start(
                        out=wvg.ap[:, dc * gw:(dc + 1) * gw],
                        in_=WvR[dc][:, h0 * 128:h0 * 128 + gw]))
                wvg.note_write(tk)

        def emit_group(gi):
            h0 = gi * p.NG
            for hh in range(p.NG):
                emit_proj_mms(h0 + hh, Kg, hh * p.T, h_T, p.T,
                              kpair(gi * p.NG + hh))
                emit_abs(2)
            for hh2 in range(p.NG):
                ki2 = (gi + 1) * p.NG + hh2
                if ki2 < nK:
                    load_w_halves(WkL, ki2, kpair(ki2))

            if gi == 0:
                load_wvg(0)

            for st in range(p.TT):
                reg = next_qreg()

                @sec("tensor")
                def _(t, w, st=st, reg=reg):
                    wvg.rd(w)
                    h_T.rd(w)
                    reg.wr(w)
                    hT = h_T.ap.rearrange("p (c t) -> p c t", c=p.DC)
                    ins = None
                    for dc in range(p.DC):
                        ins = t.matmul(reg.ap[:, 0:gw],
                                       hT[:, dc, st * 128:(st + 1) * 128],
                                       wvg.ap[:, dc * gw:(dc + 1) * gw],
                                       start=(dc == 0), stop=(dc == p.DC - 1))
                    tk = eP.inc(ins)
                    reg.note_write(tk)
                    wvg.note_read(tk)
                    h_T.note_read(tk)

                @sec("vector")
                def _(v, w, st=st, reg=reg):
                    reg.rd(w)
                    Vg.wr(w)
                    tk = eV.inc(v.tensor_copy(Vg.ap[:, st * gw:(st + 1) * gw],
                                              reg.ap[:, 0:gw]))
                    reg.note_read(tk)
                    Vg.note_write(tk)
                if st % 4 == 3:
                    emit_abs(1, which="dve")

            @sec("tensor")
            def _(t, w):
                U_b.rd(w)
                Vg.rd(w)
                psDen.wr(w)
                ins = None
                for c in range(p.TT):
                    ins = t.matmul(psDen.ap[0:p.TT, 0:gw],
                                   U_b.ap[:, c * p.TT:(c + 1) * p.TT],
                                   Vg.ap[:, c * gw:(c + 1) * gw],
                                   start=(c == 0), stop=(c == p.TT - 1),
                                   skip_group_check=True)
                tk = eP.inc(ins)
                psDen.note_write(tk)
                U_b.note_read(tk)
                Vg.note_read(tk)

            @sec("vector")
            def _(v, w):
                psDen.rd(w)
                vsufT.wr(w)
                tk = eV.inc(v.tensor_copy(vsufT.ap[:, :], psDen.ap[0:p.TT, 0:gw]))
                psDen.note_read(tk)
                vsufT.note_write(tk)

            if gi + 1 < p.NGRP:
                load_wvg(gi + 1)

            for hh in range(p.NG):
                emit_head(gi, hh)

        quarter_pool = p.TQ >= 512

        def emit_head(gi, hh):
            habs = gi * p.NG + hh
            chunks = [c for c in range(p.TT) if p.n_c[c] > 0]

            alloc = []
            ptr = 0
            for c in chunks:
                cols = p.n_c[c] * 128
                if quarter_pool:
                    ns = 1 if cols <= 512 else 2
                    if ns == 2 and ptr % 2 == 1:
                        ptr += 1
                    base = ptr % 4
                    ptr += ns
                    sbufs = sSq[base:base + ns]
                    s_ap = R01[:, base * 512:base * 512 + cols]
                    st_src = stripA if base < 2 else stripB
                    st0 = (base % 2) * 512
                    tbufs = stQ[base:base + ns]
                    t_ap = st_src.ap[:, st0:st0 + cols]
                else:
                    base = ptr % 2
                    ptr += 1
                    sbufs = [psSA if base == 0 else psSB]
                    s_ap = R01[:, base * 1024:base * 1024 + cols]
                    tb = stripA if base == 0 else stripB
                    tbufs = [tb]
                    t_ap = tb.ap[:, 0:cols]
                alloc.append((cols, sbufs, s_ap, tbufs, t_ap))

            def emit_S(ci):
                c = chunks[ci]
                cols, sbufs, s_ap, _, _ = alloc[ci]

                @sec("tensor")
                def _(t, w, c=c, cols=cols, sbufs=sbufs, s_ap=s_ap):
                    Kg.rd(w)
                    Qb.rd(w)
                    for b in sbufs:
                        b.wr(w)
                    ins = None
                    for qb in range((cols + 511) // 512):
                        wdt = min(512, cols - qb * 512)
                        ins = t.matmul(
                            s_ap[:, qb * 512:qb * 512 + wdt],
                            Kg.ap[:, hh * p.T + c * 128:hh * p.T + (c + 1) * 128],
                            Qb.ap[:, habs * p.TQ + qb * 512:habs * p.TQ + qb * 512 + wdt],
                            start=True, stop=True)
                    tk = eP.inc(ins)
                    for b in sbufs:
                        b.note_write(tk)
                    Kg.note_read(tk)
                    Qb.note_read(tk)

            emit_S(0)
            if len(chunks) > 1:
                emit_S(1)
            for ci, c in enumerate(chunks):
                cols, sbufs, s_ap, tbufs, t_ap = alloc[ci]

                @sec("vector")
                def _(v, w, c=c, cols=cols, sbufs=sbufs, s_ap=s_ap):
                    for b in sbufs:
                        b.rd(w)
                    masks_b.rd(w)
                    tk = eV.inc(v.tensor_tensor(s_ap[:, cols - 128:cols],
                                                s_ap[:, cols - 128:cols],
                                                masks_b.ap[:, c * 128:(c + 1) * 128],
                                                ALU.mult))
                    sbufs[-1].note_write(tk)
                    masks_b.note_read(tk)

                @sec("scalar")
                def _(s, w, cols=cols, sbufs=sbufs, s_ap=s_ap, tbufs=tbufs,
                      t_ap=t_ap):
                    for b in sbufs:
                        b.rd(w)
                    for b in tbufs:
                        b.wr(w)
                    tk = eA.inc(s.activation(t_ap, s_ap[:, 0:cols], AF.Exp))
                    for b in sbufs:
                        b.note_read(tk)
                    for b in tbufs:
                        b.note_write(tk)

                if ci + 2 < len(chunks):
                    emit_S(ci + 2)
                if ci % 2 == 0:
                    emit_abs(1, which="dve")

                @sec("tensor")
                def _(t, w, ci=ci, c=c, cols=cols, tbufs=tbufs, t_ap=t_ap):
                    for b in tbufs:
                        b.rd(w)
                    ones128.rd(w)
                    Vg.rd(w)
                    if ci == 0:
                        psDen.wr(w)
                        psOH.wr(w)
                    first = (ci == 0)
                    ins = None
                    for qb in range((cols + 511) // 512):
                        wdt = min(512, cols - qb * 512)
                        ins = t.matmul(psDen.ap[:, qb * 512:qb * 512 + wdt],
                                       ones128.ap[:, :],
                                       t_ap[:, qb * 512:qb * 512 + wdt],
                                       start=first, stop=False,
                                       skip_group_check=True)
                    tden = eP.inc(ins)
                    for qb in range((cols + 511) // 512):
                        wdt = min(512, cols - qb * 512)
                        ins = t.matmul(
                            psOH.ap[:, qb * 512:qb * 512 + wdt],
                            Vg.ap[:, c * gw + hh * 128:c * gw + (hh + 1) * 128],
                            t_ap[:, qb * 512:qb * 512 + wdt],
                            start=first, stop=False, skip_group_check=True)
                    tpv = eP.inc(ins)
                    psDen.note_write(tden)
                    psOH.note_write(tpv)
                    for b in tbufs:
                        b.note_read(tpv)
                    ones128.note_read(tden)
                    Vg.note_read(tpv)

            @sec("tensor")
            def _(t, w):
                vsufT.rd(w)
                G_b.rd(w)
                ins = None
                for qb in range(max(1, p.TQ // 512)):
                    wdt = min(512, p.TQ)
                    ins = t.matmul(psOH.ap[:, qb * 512:qb * 512 + wdt],
                                   vsufT.ap[0:p.TT, hh * 128:(hh + 1) * 128],
                                   G_b.ap[0:p.TT, qb * 512:qb * 512 + wdt],
                                   start=False, stop=True, skip_group_check=True)
                tk = eP.inc(ins)
                psOH.note_write(tk)
                vsufT.note_read(tk)
                G_b.note_read(tk)

            @sec("vector")
            def _(v, w):
                psDen.rd(w)
                counts_b.rd(w)
                den_f.wr(w)
                tk = eV.inc(v.tensor_add(den_f.ap[:, :], psDen.ap[0:1, 0:p.TQ],
                                         counts_b.ap[:, :]))
                den_f.note_write(tk)
                psDen.note_read(tk)
                counts_b.note_read(tk)

            @sec("scalar")
            def _(s, w):
                den_f.rd(w)
                t1 = eA.inc(s.activation(den_f.ap[:, :], den_f.ap[:, :], AF.Ln))
                den_f.note_write(t1)
                w.wait(t1)
                t2 = eA.inc(s.activation(den_f.ap[:, :], den_f.ap[:, :],
                                         AF.Exp, scale=-1.0))
                den_f.note_write(t2)

            @sec("gpsimd")
            def _(e, w, habs=habs):
                den_f.rd(w)
                tok = qof(den_f, "o").inc(e.dma_start(
                    out=den_spill[habs:habs + 1, :], in_=den_f.ap[:, :]))
                den_f.note_read(tok)
                den_store_toks[habs] = tok

            @sec("vector")
            def _(v, w):
                psOH.rd(w)
                outheadsH[habs].wr(w)
                tk = eV.inc(v.tensor_copy(
                    outheadsH[habs].ap[:, :],
                    psOH.ap[:, 0:p.TQ]))
                outheadsH[habs].note_write(tk)
                psOH.note_read(tk)

        den_store_toks = {}
        for gi in range(p.NGRP):
            emit_group(gi)

        for habs in range(p.H):
            breg = psDen if habs % 2 == 0 else psOH
            dfx = den_fA if habs % 2 == 0 else den_fB
            dma_in("sync", dfx, dfx.ap[:, :], den_spill[habs:habs + 1, :],
                   deps=(den_store_toks[habs],))

            @sec("tensor")
            def _(t, w, habs=habs, breg=breg, dfx=dfx):
                dfx.rd(w)
                onesrow.rd(w)
                breg.wr(w)
                ins = None
                for qb in range(max(1, p.TQ // 512)):
                    wdt = min(512, p.TQ)
                    ins = t.matmul(breg.ap[:, qb * 512:qb * 512 + wdt],
                                   onesrow.ap[0:1, :],
                                   dfx.ap[0:1, qb * 512:qb * 512 + wdt],
                                   start=True, stop=True, skip_group_check=True)
                tk = eP.inc(ins)
                breg.note_write(tk)
                onesrow.note_read(tk)
                dfx.note_read(tk)

            @sec("vector")
            def _(v, w, habs=habs, breg=breg):
                breg.rd(w)
                outheadsH[habs].wr(w)
                oh = outheadsH[habs].ap[:, :]
                tk = eV.inc(v.tensor_tensor(oh, oh, breg.ap[:, 0:p.TQ],
                                            ALU.mult))
                outheadsH[habs].note_write(tk)
                breg.note_read(tk)

        if debug:
            for _h in range(p.H):
                dma_store("gpsimd", dbg_oh[:, _h * p.TQ:(_h + 1) * p.TQ],
                          outheadsH[_h], outheadsH[_h].ap[:, :])
            dma_store("gpsimd", dbg_q, Qb, Qb.ap[:, :])
            dma_store("gpsimd", dbg_k, Kg, Kg.ap[:, :])
            dma_store("gpsimd", dbg_v, Vg, Vg.ap[:, :])

        emit_abs(len(abs_sched))

        SIGMA = float(2.0 ** 19)

        @sec("vector")
        def _(v, w):
            absP.rd(w)
            bc_sb.wr(w)
            q_ap = bc_sb.ap[:, 0:p.EC]
            r_ap = bc_sb.ap[:, 64:64 + p.EC]
            tk = None
            for wi in range(3):
                cols = absP.ap[:, wi * 72:wi * 72 + p.EC]
                t1 = eV.inc(v.tensor_scalar(q_ap, cols, SIGMA, SIGMA,
                                            ALU.add, ALU.subtract))
                w.wait(t1)
                t2 = eV.inc(v.tensor_tensor(r_ap, cols, q_ap, ALU.subtract))
                w.wait(t2)
                t3 = eV.inc(v.reduce_sum(absP.ap[:, wi * 72 + 70:wi * 72 + 71],
                                         q_ap, axis=AX.X))
                tk = eV.inc(v.reduce_sum(absP.ap[:, wi * 72 + 71:wi * 72 + 72],
                                         r_ap, axis=AX.X))
                w.wait(tk)
            absP.note_write(tk)
            bc_sb.note_write(tk)

        @sec("tensor")
        def _(t, w):
            absP.rd(w)
            onescol.rd(w)
            psSA.wr(w)
            ap6 = absP.ap.rearrange("p (w c) -> p w c", w=3)[:, :, 70:72]
            tk = eP.inc(t.matmul(psSA.ap[0:1, 0:6], onescol.ap[:, :], ap6,
                                 start=True, stop=True, skip_group_check=True))
            psSA.note_write(tk)
            absP.note_read(tk)
            onescol.note_read(tk)

        @sec("vector")
        def _(v, w):
            psSA.rd(w)
            sc_misc.wr(w)
            epsp.rd(w)
            t0 = eV.inc(v.tensor_copy(sc_misc.ap[:, 2:8], psSA.ap[0:1, 0:6]))
            w.wait(t0)
            qs = sc_misc.ap[0:1, 2:8].rearrange("p (w two) -> p w two", two=2)
            t1 = eV.inc(v.tensor_tensor(sc_misc.ap[:, 0:3], qs[:, :, 0],
                                        qs[:, :, 1], ALU.add))
            w.wait(t1)
            t2 = eV.inc(v.tensor_single_scalar(sc_misc.ap[:, 0:3],
                                               sc_misc.ap[:, 0:3],
                                               1.0 / (p.E * p.D), ALU.mult))
            w.wait(t2)
            t3 = eV.inc(v.tensor_single_scalar(sc_misc.ap[:, 3:6],
                                               sc_misc.ap[:, 0:3],
                                               epsp.ap[0:1, 0:1], ALU.mult))
            w.wait(t3)
            t4 = eV.inc(v.scalar_tensor_tensor(sc_misc.ap[:, 0:3],
                                               sc_misc.ap[:, 3:6], -12.5,
                                               sc_misc.ap[:, 0:3],
                                               ALU.mult, ALU.add))
            w.wait(t4)
            tk = eV.inc(v.tensor_single_scalar(sc_misc.ap[:, 0:3],
                                               sc_misc.ap[:, 0:3],
                                               p.QEPS, ALU.add))
            sc_misc.note_write(tk)
            psSA.note_read(tk)
            epsp.note_read(tk)

        @sec("tensor")
        def _(t, w):
            sc_misc.rd(w)
            onesrow.rd(w)
            psSB.wr(w)
            tk = eP.inc(t.matmul(psSB.ap[:, 0:3], onesrow.ap[0:1, :],
                                 sc_misc.ap[0:1, 0:3], start=True, stop=True,
                                 skip_group_check=True))
            psSB.note_write(tk)
            sc_misc.note_read(tk)
            onesrow.note_read(tk)

        @sec("vector")
        def _(v, w):
            psSB.rd(w)
            epsp.rd(w)
            for buf, col in ((s_up, 0), (s_gate, 1), (s_down, 2)):
                buf.wr(w)
                for dst, mul in ((0, -1.0), (1, 0.5), (2, -0.5)):
                    tk = eV.inc(v.tensor_single_scalar(
                        buf.ap[:, dst:dst + 1], psSB.ap[:, col:col + 1],
                        mul, ALU.mult))
                w.wait(tk)
                tk = eV.inc(v.tensor_single_scalar(
                    buf.ap[:, 3:4], psSB.ap[:, col:col + 1],
                    epsp.ap[:, 0:1], ALU.mult))
                w.wait(tk)
                tk = eV.inc(v.tensor_single_scalar(
                    buf.ap[:, 4:5], buf.ap[:, 3:4], -1.0, ALU.mult))
                w.wait(tk)
                t6 = eV.inc(v.tensor_tensor(buf.ap[:, 6:7], buf.ap[:, 1:2],
                                            buf.ap[:, 3:4], ALU.add))
                w.wait(t6)
                tk = eV.inc(v.tensor_single_scalar(buf.ap[:, 5:6],
                                                   buf.ap[:, 6:7], -1.0,
                                                   ALU.mult))
                buf.note_write(tk)
                psSB.note_read(tk)
            epsd.wr(w)
            s_down.rd(w)
            t1 = eV.inc(v.tensor_tensor(epsd.ap[:, :], s_down.ap[:, 0:1],
                                        s_down.ap[:, 0:1], ALU.mult))
            s_down.note_read(t1)
            w.wait(t1)
            t2 = eV.inc(v.reciprocal(epsd.ap[:, :], epsd.ap[:, :]))
            w.wait(t2)
            epsb.rd(w)
            t3 = eV.inc(v.tensor_tensor(epsd.ap[:, :], epsd.ap[:, :],
                                        epsb.ap[:, :], ALU.mult))
            epsb.note_read(t3)
            epsd.note_write(t3)

        if debug:
            @sec("gpsimd")
            def _(g, w):
                s_up.rd(w); s_gate.rd(w); s_down.rd(w); epsd.rd(w)
                tk = None
                for i, b in enumerate((s_up, s_gate, s_down)):
                    tk = qof(b, "dbg").inc(g.dma_start(out=dbg_scales[:, i*3:(i+1)*3], in_=b.ap[:, 0:3]))
                    b.note_read(tk)
                tk = qof(epsd, "dbg").inc(g.dma_start(out=dbg_scales[:, 9:10], in_=epsd.ap[:, :], allow_slow_non_contiguous=True))
                epsd.note_read(tk)


        def emit_wo():
            ndb = max(1, p.D // 512)
            wdt = min(512, p.D)
            for db in range(ndb):
                for dc in range(p.DC):
                    wslab = hbufA if dc % 2 == 0 else hbufB
                    dma_in("gpsimd", wslab, wslab.ap[:, 0:wdt],
                           WoR[dc][:, db * 512:db * 512 + wdt])

                    @sec("tensor")
                    def _(t, w, db=db, dc=dc, wslab=wslab):
                        wslab.rd(w)
                        outheadsH[dc].rd(w)
                        ins = None
                        for qk in range(p.QT):
                            reg = qkv8[qk % 8]
                            if dc == 0:
                                reg.wr(w)
                            ins = t.matmul(
                                reg.ap[:, 0:wdt],
                                outheadsH[dc].ap[:, qk * 128:(qk + 1) * 128],
                                wslab.ap[:, 0:wdt],
                                start=(dc == 0), stop=(dc == p.DC - 1),
                                skip_group_check=True)
                        tk = eP.inc(ins)
                        wslab.note_read(tk)
                        outheadsH[dc].note_read(tk)
                        for qk in range(p.QT):
                            qkv8[qk % 8].note_write(tk)

                for qk in range(p.QT):
                    reg = qkv8[qk % 8]

                    @sec("vector")
                    def _(v, w, qk=qk, reg=reg, db=db):
                        reg.rd(w)
                        attn_outQ[qk].wr(w)
                        tk = eV.inc(v.tensor_copy(
                            attn_outQ[qk].ap[:, db * 512:db * 512 + wdt],
                            reg.ap[:, 0:wdt]))
                        reg.note_read(tk)
                        attn_outQ[qk].note_write(tk)

        emit_wo()
        if debug:
            for _q in range(p.QT):
                dma_store("gpsimd", dbg_ao[:, _q * p.D:(_q + 1) * p.D],
                          attn_outQ[_q], attn_outQ[_q].ap[:, :])

        def emit_x1_ln2():
            n = p.QT

            def ld(qk):
                xt = xrot[qk % 3][0]
                load_xtile(xt, xq[qk * 128:(qk + 1) * 128, :])

            def st_attn(qk):
                _, mv, rstd = xrot[qk % 3]
                emit_ln_stats(attn_outQ[qk],
                              lambda qk=qk: attn_outQ[qk].ap[:, :],
                              mv, rstd, epsb)

            def norm_attn(qk):
                _, mv, rstd = xrot[qk % 3]
                emit_normalize(attn_outQ[qk],
                               lambda qk=qk: attn_outQ[qk].ap[:, :],
                               mv, rstd, attn_outQ[qk],
                               attn_outQ[qk].ap[:, :])

            ld(0)
            st_attn(0)
            norm_attn(0)
            if n > 1:
                ld(1)
                st_attn(1)
            for qk in range(n):
                xt, mv, rstd = xrot[qk % 3]
                if qk + 1 < n:
                    norm_attn(qk + 1)

                @sec("vector")
                def _(v, w, qk=qk, xt=xt):
                    xt.rd(w)
                    attn_outQ[qk].rd(w)
                    tk = eV.inc(v.tensor_add(xt.ap[:, :], xt.ap[:, :],
                                             attn_outQ[qk].ap[:, :]))
                    xt.note_write(tk)
                    attn_outQ[qk].note_read(tk)

                if qk + 2 < n:
                    st_attn(qk + 2)
                    ld(qk + 2)
                dma_store("sync", x1_spill[qk * 128:(qk + 1) * 128, 0:p.D // 2],
                          xt, xt.ap[:, 0:p.D // 2])
                dma_store("gpsimd", x1_spill[qk * 128:(qk + 1) * 128, p.D // 2:p.D],
                          xt, xt.ap[:, p.D // 2:p.D], tag="o2")
                emit_ln_stats(xt, lambda xt=xt: xt.ap, mv, rstd, epsb)
                emit_normalize(xt, lambda xt=xt: xt.ap, mv, rstd, xt, xt.ap[:, :])
                emit_transposes(xt, h2_T, p.TQ, qk * 128)

        emit_x1_ln2()
        if debug:
            dma_store("gpsimd", dbg_h2t, h2_T, h2_T.ap[:, :])

        def quant_sign(fslab, scales, qm, tern_buf, tern_off):
            dst = tern_buf.ap[:, tern_off:tern_off + p.D // 2]

            @sec("scalar")
            def _(sc, w):
                fslab.rd(w)
                scales.rd(w)
                tern_buf.wr(w)
                t1 = eA.inc(sc.activation(dst, fslab.ap[:, :],
                                          AF.Sign, bias=scales.ap[:, 5:6]))
                tern_buf.note_write(t1)
                qm.wr(w)
                t2 = eA.inc(sc.activation(qm.ap[:, 0:p.D // 2], fslab.ap[:, :],
                                          AF.Sign, bias=scales.ap[:, 6:7]))
                qm.note_write(t2)
                fslab.note_read(t2)
                scales.note_read(t2)

            @sec("vector")
            def _(v, w):
                qm.rd(w)
                tern_buf.rd(w)
                tern_buf.wr(w)
                tk = eV.inc(v.tensor_add(dst, dst, qm.ap[:, 0:p.D // 2]))
                tern_buf.note_write(tk)
                qm.note_read(tk)

        def quant3(fslab, scales, qm, tern_buf, tern_off, exact):
            dst = tern_buf.ap[:, tern_off:tern_off + p.D // 2]

            @sec("vector")
            def _(v, w):
                fslab.rd(w)
                scales.rd(w)
                tern_buf.wr(w)
                if exact:
                    t1 = eV.inc(v.tensor_scalar(dst, fslab.ap[:, :],
                                                scales.ap[:, 1:2],
                                                scales.ap[:, 4:5],
                                                ALU.add, ALU.is_lt))
                else:
                    t1 = eV.inc(v.tensor_single_scalar(dst, fslab.ap[:, :],
                                                       scales.ap[:, 5:6],
                                                       ALU.is_lt))
                tern_buf.note_write(t1)
                qm.wr(w)
                if exact:
                    t2 = eV.inc(v.tensor_scalar(qm.ap[:, 0:p.D // 2],
                                                fslab.ap[:, :],
                                                scales.ap[:, 1:2],
                                                scales.ap[:, 3:4],
                                                ALU.subtract, ALU.is_gt))
                else:
                    t2 = eV.inc(v.tensor_single_scalar(qm.ap[:, 0:p.D // 2],
                                                       fslab.ap[:, :],
                                                       scales.ap[:, 6:7],
                                                       ALU.is_gt))
                qm.note_write(t2)
                w.wait(t1, t2)
                tern_buf.wr(w)
                if exact:
                    tk = eV.inc(v.tensor_tensor(dst, qm.ap[:, 0:p.D // 2], dst,
                                                ALU.subtract))
                else:
                    tk = eV.inc(v.tensor_tensor(dst, dst, qm.ap[:, 0:p.D // 2],
                                                ALU.subtract))
                tern_buf.note_write(tk)
                qm.note_read(tk)
                fslab.note_read(t2)
                scales.note_read(t2)

        def ffn_quant_stage(ec, part="all"):
            ecl = ec % p.SUPER
            tu = tUpA if ec % 2 == 0 else tUpB
            tg = tGtA if ec % 2 == 0 else tGtB
            hD = p.D // 2
            if part in ("all", "pre"):
                for half, fs, qm in ((0, fsUpA, qmA), (1, fsUpB, qmC)):
                    dma_in("sync", fs, fs.ap[:, :],
                           UpL[ec][:, half * hD:(half + 1) * hD])
                    quant_sign(fs, s_up, qm, tu, half * hD)
                for half, fs, qm in ((0, fsGtA, qmA), (1, fsGtB, qmC)):
                    dma_in("gpsimd", fs, fs.ap[:, :],
                           GateL[ec][:, half * hD:(half + 1) * hD])
                    quant_sign(fs, s_gate, qm, tg, half * hD)
                if part == "pre":
                    for half, fs in ((0, fsDnA), (1, fsDnB)):
                        dma_in("gpsimd", fs, fs.ap[:, :],
                               DownR[ec][:, half * hD:(half + 1) * hD])
            if part in ("all", "post"):
                for half, fs in ((0, fsDnA), (1, fsDnB)):
                    if part == "all":
                        dma_in("gpsimd", fs, fs.ap[:, :],
                               DownR[ec][:, half * hD:(half + 1) * hD])
                    quant3(fs, s_down, qmB, ternDown, ecl * p.D + half * hD,
                           exact=True)

        def emit_ffn():
            nth = max(1, p.TQ // 512)
            wdt = min(512, p.TQ)
            ffn_quant_stage(0)
            for sup in range(p.NSUP):
                for ecl in range(p.SUPER):
                    ec = sup * p.SUPER + ecl
                    tu = tUpA if ec % 2 == 0 else tUpB
                    tg = tGtA if ec % 2 == 0 else tGtB
                    upreg = psSA if ec % 2 == 0 else psSB
                    gtreg = psOH if ec % 2 == 0 else psDen
                    if ec + 1 < p.EC and (ec + 1) % p.SUPER != 0:
                        ffn_quant_stage(ec + 1)

                    for th in range(nth):
                        @sec("tensor")
                        def _(t, w, tu=tu, tg=tg, th=th, upreg=upreg, gtreg=gtreg):
                            tu.rd(w)
                            tg.rd(w)
                            h2_T.rd(w)
                            if th == 0:
                                upreg.wr(w)
                                gtreg.wr(w)
                            h2 = h2_T.ap.rearrange("p (c t) -> p c t", c=p.DC)
                            ins = None
                            for dc in range(p.DC):
                                ins = t.matmul(upreg.ap[:, th * 512:th * 512 + wdt],
                                               tu.ap[:, dc * 128:(dc + 1) * 128],
                                               h2[:, dc, th * 512:th * 512 + wdt],
                                               start=(dc == 0), stop=(dc == p.DC - 1),
                                               skip_group_check=True)
                            tku = eP.inc(ins)
                            for dc in range(p.DC):
                                ins = t.matmul(gtreg.ap[:, th * 512:th * 512 + wdt],
                                               tg.ap[:, dc * 128:(dc + 1) * 128],
                                               h2[:, dc, th * 512:th * 512 + wdt],
                                               start=(dc == 0), stop=(dc == p.DC - 1),
                                               skip_group_check=True)
                            tkg = eP.inc(ins)
                            upreg.note_write(tku)
                            gtreg.note_write(tkg)
                            tu.note_read(tkg)
                            tg.note_read(tkg)
                            h2_T.note_read(tkg)

                    @sec("scalar")
                    def _(s, w, upreg=upreg):
                        upreg.rd(w)
                        s_up.rd(w)
                        silu_t.wr(w)
                        tk = eA.inc(s.activation(silu_t.ap[:, 0:p.TQ],
                                                 upreg.ap[:, 0:p.TQ],
                                                 AF.Silu, scale=s_up.ap[:, 1:2]))
                        silu_t.note_write(tk)
                        upreg.note_read(tk)
                        s_up.note_read(tk)

                    @sec("vector")
                    def _(v, w, ecl=ecl, gtreg=gtreg):
                        gtreg.rd(w)
                        silu_t.rd(w)
                        s_gate.rd(w)
                        hidden.wr(w)
                        tk = eV.inc(v.scalar_tensor_tensor(
                            hidden.ap[:, ecl * p.TQ:(ecl + 1) * p.TQ],
                            gtreg.ap[:, 0:p.TQ], s_gate.ap[:, 1:2],
                            silu_t.ap[:, 0:p.TQ], ALU.mult, ALU.mult))
                        hidden.note_write(tk)
                        gtreg.note_read(tk)
                        silu_t.note_read(tk)
                        s_gate.note_read(tk)

                if sup + 1 < p.NSUP:
                    ffn_quant_stage((sup + 1) * p.SUPER, part="pre")

                for tki in range(p.TQ // 128):
                    reg = downA if tki % 2 == 0 else downB
                    ndb = max(1, p.D // 512)
                    dwdt = min(512, p.D)

                    @sec("tensor")
                    def _(t, w, sup=sup, tki=tki, reg=reg, ndb=ndb, dwdt=dwdt):
                        hidden.rd(w)
                        ternDown.rd(w)
                        reg.wr(w)
                        ins = None
                        for ecl in range(p.SUPER):
                            for db in range(ndb):
                                ins = t.matmul(
                                    reg.ap[:, db * 512:db * 512 + dwdt],
                                    hidden.ap[:, ecl * p.TQ + tki * 128:
                                              ecl * p.TQ + (tki + 1) * 128],
                                    ternDown.ap[:, ecl * p.D + db * 512:
                                                ecl * p.D + db * 512 + dwdt],
                                    start=(ecl == 0), stop=(ecl == p.SUPER - 1),
                                    skip_group_check=True)
                        tk = eP.inc(ins)
                        reg.note_write(tk)
                        hidden.note_read(tk)
                        ternDown.note_read(tk)

                    @sec("vector")
                    def _(v, w, sup=sup, tki=tki, reg=reg):
                        reg.rd(w)
                        down_acc.wr(w)
                        dst = down_acc.ap[:, tki * p.D:(tki + 1) * p.D]
                        if sup == 0:
                            ins = v.tensor_copy(dst, reg.ap[:, 0:p.D])
                        else:
                            ins = v.tensor_add(dst, dst, reg.ap[:, 0:p.D])
                        tk = eV.inc(ins)
                        reg.note_read(tk)
                        down_acc.note_write(tk)

                if sup + 1 < p.NSUP:
                    ffn_quant_stage((sup + 1) * p.SUPER, part="post")

        emit_ffn()
        if debug:
            dma_store("gpsimd", dbg_down, down_acc, down_acc.ap[:, 0:(p.TQ // 128) * p.D])

        def emit_final():
            n = p.TQ // 128

            def ld(qk):
                xt = xrot[qk % 3][0]
                load_xtile(xt, x1_spill[qk * 128:(qk + 1) * 128, :])

            def st_down(qk):
                _, mv, rstd = xrot[qk % 3]
                emit_ln_stats(down_acc,
                              lambda qk=qk: down_acc.ap[:, qk * p.D:(qk + 1) * p.D],
                              mv, rstd, epsd)

            def norm_down(qk):
                _, mv, rstd = xrot[qk % 3]
                emit_normalize(down_acc,
                               lambda qk=qk: down_acc.ap[:, qk * p.D:(qk + 1) * p.D],
                               mv, rstd, down_acc,
                               down_acc.ap[:, qk * p.D:(qk + 1) * p.D])

            ld(0)
            st_down(0)
            norm_down(0)
            if n > 1:
                ld(1)
                st_down(1)
            for qk in range(n):
                xt, mv, rstd = xrot[qk % 3]

                @sec("vector")
                def _(v, w, qk=qk, xt=xt):
                    xt.rd(w)
                    down_acc.rd(w)
                    tk = eV.inc(v.tensor_add(xt.ap[:, :], xt.ap[:, :],
                                             down_acc.ap[:, qk * p.D:(qk + 1) * p.D]))
                    xt.note_write(tk)
                    down_acc.note_read(tk)

                if qk + 1 < n:
                    norm_down(qk + 1)
                if qk + 2 < n:
                    st_down(qk + 2)
                    ld(qk + 2)

                dma_store("sync", out_d[qk * 128:(qk + 1) * 128, 0:p.D // 2],
                          xt, xt.ap[:, 0:p.D // 2])
                dma_store("gpsimd", out_d[qk * 128:(qk + 1) * 128, p.D // 2:p.D],
                          xt, xt.ap[:, p.D // 2:p.D], tag="o2")

        emit_final()

        @sec("sync")
        def _(sy, w):
            for ctr in dma_ctrs.values():
                if ctr.cnt:
                    sy.wait_ge(ctr.sem, ctr.cnt)

        def _replay(rec):
            def cb(eng):
                for e in rec.calls:
                    if e[0] == "w":
                        eng.wait_ge(e[1], e[2])
                    else:
                        _, m, a, k, incs = e
                        ins = getattr(eng, m)(*a, **k)
                        for sem, step in incs:
                            ins.then_inc(sem, step)
            return cb

        for engname in ("sync", "gpsimd", "vector", "scalar", "tensor"):
            getattr(blk, engname)(_replay(recs[engname]))

    return nc


def make_in_maps(par, x, Wq, Wk, Wv, Wo, Wup, Wgate, Wdown):
    lay = host_layouts(par, Wq, Wk, Wv, Wo, Wup, Wgate, Wdown)
    WqL, WkL, WvR, WoR, UpL, GateL, DownR = lay
    consts = [core_consts(par, parity) for parity in (0, 1)]
    in_maps, perms = [], []
    for core in range(2 * par.B):
        b, parity = core // 2, core % 2
        own = par.owned[parity]
        perm = np.concatenate([np.arange(g * 128, (g + 1) * 128) for g in own])
        perms.append(perm)
        cc = consts[parity]
        in_maps.append(dict(
            xs=x[b], xq=np.ascontiguousarray(x[b][perm]),
            WqL=WqL, WkL=WkL, WvR=WvR, WoR=WoR,
            UpL=UpL, GateL=GateL, DownR=DownR,
            masks=cc["masks"], Gmat=cc["Gmat"], counts=cc["counts"],
            Umat=cc["Umat"], ident=cc["ident"]))
    return in_maps, perms


def kernel(L0_attn_x, L0_attn_Wq, L0_attn_Wk, L0_attn_Wv, L0_attn_Wo,
           L0_ffn_W_up, L0_ffn_W_gate, L0_ffn_W_down):
    from concourse.bass_utils import run_bass_kernel_spmd

    par = Par()
    x = np.ascontiguousarray(np.asarray(L0_attn_x, dtype=np.float32))
    in_maps, perms = make_in_maps(
        par, x,
        np.asarray(L0_attn_Wq, np.float32), np.asarray(L0_attn_Wk, np.float32),
        np.asarray(L0_attn_Wv, np.float32), np.asarray(L0_attn_Wo, np.float32),
        np.asarray(L0_ffn_W_up, np.float32), np.asarray(L0_ffn_W_gate, np.float32),
        np.asarray(L0_ffn_W_down, np.float32))
    nc = build_program(par)
    res = run_bass_kernel_spmd(nc, in_maps, core_ids=list(range(8)))
    shard0 = np.asarray(res.results[0]["out"])
    out = np.empty((par.B, par.T, par.D), shard0.dtype)
    for core in range(2 * par.B):
        b = core // 2
        out[b][perms[core]] = np.asarray(res.results[core]["out"])
    return out


# version 28
# speedup vs baseline: 1.0684x; 1.0684x over previous
import math

import numpy as np
import ml_dtypes

import concourse.bass as bass
import concourse.mybir as mybir

F32 = mybir.dt.float32
BF16 = mybir.dt.bfloat16
AF = mybir.ActivationFunctionType
ALU = mybir.AluOpType
AX = mybir.AxisListType

BF = ml_dtypes.bfloat16


class Par:
    def __init__(self, T=2048, D=2048, E=8192, B=4, NG=2, SUPER=8):
        self.T, self.D, self.E, self.B = T, D, E, B
        self.H = D // 128
        self.TT = T // 128
        self.QT = self.TT // 2
        self.TQ = T // 2
        self.DC = D // 128
        self.EC = E // 128
        self.NG = NG
        assert self.H % NG == 0 and NG * 128 <= 512
        self.NGRP = self.H // NG
        self.SUPER = min(SUPER, self.EC)
        assert self.EC % self.SUPER == 0
        self.NSUP = self.EC // self.SUPER
        self.EPS = 1e-5
        self.QEPS = 1e-8
        self.ISQT = 1.0 / math.sqrt(T)
        self.owned = {0: list(range(self.TT - 1, 0, -2)),
                      1: list(range(self.TT - 2, -1, -2))}
        self.n_c = [len([g for g in self.owned[0] if g >= c]) for c in range(self.TT)]

    def prefix(self, parity, c):
        own = self.owned[parity]
        lst = [g for g in own if g >= c]
        if len(lst) < self.n_c[c]:
            lst = lst + [max(g for g in own if g < c)]
        assert len(lst) == self.n_c[c]
        return lst


def core_consts(par: Par, parity: int):
    TT, TQ = par.TT, par.TQ
    own = par.owned[parity]
    gmax = {}
    for g in own:
        gm = g
        if g + 1 < TT and par.n_c[g + 1] > 0 and par.prefix(parity, g + 1)[-1] == g:
            gm = g + 1
        gmax[g] = gm
    masks = np.ones((TT, 128, 128), np.float32)
    triu = (np.arange(128)[:, None] <= np.arange(128)[None, :]).astype(np.float32)
    for c in range(TT):
        if par.n_c[c] == 0:
            continue
        last = par.prefix(parity, c)[-1]
        if last == c:
            masks[c] = triu
        elif last < c:
            masks[c] = 0.0
    G = np.zeros((TT, TQ), np.float32)
    counts = np.zeros((1, TQ), np.float32)
    for k, g in enumerate(own):
        G[gmax[g], k * 128:(k + 1) * 128] = 1.0
        counts[0, k * 128:(k + 1) * 128] = 128.0 * (TT - 1 - gmax[g])
    U = np.zeros((TT, 128, TT), np.float32)
    for c in range(TT):
        U[c, :, :c] = 1.0
    return dict(masks=masks.astype(BF), Gmat=G.astype(BF),
                counts=counts.astype(BF), Umat=U.astype(BF),
                ident=np.eye(128, dtype=np.float32))


def host_layouts(par, Wq, Wk, Wv, Wo, Wup, Wgate, Wdown):
    DC, EC, D = par.DC, par.EC, par.D
    def colmajor(Wm):
        a = Wm.reshape(DC, 128, DC, 128)
        return np.ascontiguousarray(a.transpose(2, 1, 0, 3)).reshape(DC, 128, D)
    def up_tiles(Wm):
        a = Wm.reshape(EC, 128, DC, 128)
        return np.ascontiguousarray(a.transpose(0, 3, 2, 1)).reshape(EC, 128, D)
    return (colmajor(Wq), colmajor(Wk),
            np.ascontiguousarray(Wv.reshape(DC, 128, D)),
            np.ascontiguousarray(Wo.reshape(DC, 128, D)),
            up_tiles(Wup), up_tiles(Wgate),
            np.ascontiguousarray(np.ascontiguousarray(Wdown.T).reshape(EC, 128, D)))


class Tok:
    __slots__ = ("sem", "val")
    def __init__(self, sem, val):
        self.sem, self.val = sem, val


class Counter:
    def __init__(self, nc, name, step=1):
        self.sem = nc.alloc_semaphore(name)
        self.cnt = 0
        self.step = step

    def inc(self, instr):
        instr.then_inc(self.sem, self.step)
        self.cnt += self.step
        return Tok(self.sem, self.cnt)


class Waiter:
    def __init__(self):
        self.eng = None
        self.hw = {}

    def wait(self, *toks):
        for t in toks:
            if t is None:
                continue
            k = t.sem.num
            if self.hw.get(k, 0) < t.val:
                self.eng.wait_ge(t.sem, t.val)
                self.hw[k] = t.val


class Buf:
    __slots__ = ("ap", "w", "reads", "others")
    def __init__(self, ap):
        self.ap = ap
        self.w = {}
        self.reads = {}
        self.others = []

    def _merge(self, d, tok):
        cur = d.get(tok.sem.num)
        if cur is None or cur.val < tok.val:
            d[tok.sem.num] = tok

    def rd(self, waiter):
        for b in (self,) + tuple(self.others):
            waiter.wait(*b.w.values())

    def wr(self, waiter):
        for b in (self,) + tuple(self.others):
            waiter.wait(*b.w.values())
            waiter.wait(*b.reads.values())

    def note_read(self, tok):
        self._merge(self.reads, tok)

    def note_write(self, tok):
        self._merge(self.w, tok)
        self.reads = {}


def link(*bufs):
    for b in bufs:
        b.others = [o for o in bufs if o is not b]


def link_pair(a, b):
    if b not in a.others:
        a.others.append(b)
    if a not in b.others:
        b.others.append(a)


def build_program(p: Par, debug=False):
    nc = bass.Bass("TRN2", target_bir_lowering=False)

    xs = nc.dram_tensor("xs", [p.T, p.D], F32, kind="ExternalInput").ap()
    xq = nc.dram_tensor("xq", [p.TQ, p.D], F32, kind="ExternalInput").ap()
    WqL = nc.dram_tensor("WqL", [p.DC, 128, p.D], F32, kind="ExternalInput").ap()
    WkL = nc.dram_tensor("WkL", [p.DC, 128, p.D], F32, kind="ExternalInput").ap()
    WvR = nc.dram_tensor("WvR", [p.DC, 128, p.D], F32, kind="ExternalInput").ap()
    WoR = nc.dram_tensor("WoR", [p.DC, 128, p.D], F32, kind="ExternalInput").ap()
    UpL = nc.dram_tensor("UpL", [p.EC, 128, p.D], F32, kind="ExternalInput").ap()
    GateL = nc.dram_tensor("GateL", [p.EC, 128, p.D], F32, kind="ExternalInput").ap()
    DownR = nc.dram_tensor("DownR", [p.EC, 128, p.D], F32, kind="ExternalInput").ap()
    masks_in = nc.dram_tensor("masks", [p.TT, 128, 128], BF16, kind="ExternalInput").ap()
    G_in = nc.dram_tensor("Gmat", [p.TT, p.TQ], BF16, kind="ExternalInput").ap()
    counts_in = nc.dram_tensor("counts", [1, p.TQ], BF16, kind="ExternalInput").ap()
    U_in = nc.dram_tensor("Umat", [p.TT, 128, p.TT], BF16, kind="ExternalInput").ap()
    ident_in = nc.dram_tensor("ident", [128, 128], F32, kind="ExternalInput").ap()
    out_d = nc.dram_tensor("out", [p.TQ, p.D], F32, kind="ExternalOutput").ap()
    if debug:
        dbg_scales = nc.dram_tensor("dbg_scales", [128, 16], F32, kind="ExternalOutput").ap()
        dbg_ternup = nc.dram_tensor("dbg_ternup", [128, p.D], F32, kind="ExternalOutput").ap()
        dbg_terngate = nc.dram_tensor("dbg_terngate", [128, p.D], F32, kind="ExternalOutput").ap()
        dbg_terndown = nc.dram_tensor("dbg_terndown", [128, p.SUPER * p.D], F32, kind="ExternalOutput").ap()
        dbg_hidden = nc.dram_tensor("dbg_hidden", [128, p.SUPER * p.TQ], F32, kind="ExternalOutput").ap()
        dbg_down = nc.dram_tensor("dbg_down", [128, (p.TQ // 128) * p.D], F32, kind="ExternalOutput").ap()
        dbg_h2t = nc.dram_tensor("dbg_h2t", [128, p.DC * p.TQ], F32, kind="ExternalOutput").ap()
        dbg_oh = nc.dram_tensor("dbg_oh", [128, p.H * p.TQ], F32, kind="ExternalOutput").ap()
        dbg_q = nc.dram_tensor("dbg_q", [128, p.H * p.TQ], F32, kind="ExternalOutput").ap()
        dbg_k = nc.dram_tensor("dbg_k", [128, p.NG * p.T], F32, kind="ExternalOutput").ap()
        dbg_v = nc.dram_tensor("dbg_v", [128, p.TT * p.NG * 128], F32, kind="ExternalOutput").ap()
        dbg_den = nc.dram_tensor("dbg_den", [1, p.TQ], F32, kind="ExternalOutput").ap()
        dbg_ao = nc.dram_tensor("dbg_ao", [128, p.QT * p.D], F32, kind="ExternalOutput").ap()
    x1_spill = nc.dram_tensor("x1_spill", [p.TQ, p.D], F32, kind="Internal").ap()
    den_spill = nc.dram_tensor("den_spill", [p.H, p.TQ], F32, kind="Internal").ap()

    def alias_at(handle, name, shape, dt, delta=0):
        addr = nc.lookup_mloc(handle).addr + delta
        return Buf(nc.alloc_sbuf_tensor_at(name, list(shape), dt, offset=addr).ap())

    gw = p.NG * 128
    a_cols = p.DC * p.T
    arena_a_h = nc.alloc_sbuf_tensor("arena_a", [128, a_cols], BF16)
    h_T = Buf(arena_a_h.ap())
    attn_out_h = nc.alloc_sbuf_tensor_at(
        "attn_out", [128, p.QT * p.D], F32,
        offset=nc.lookup_mloc(arena_a_h).addr)
    down_acc = alias_at(arena_a_h, "down_acc", (128, (p.TQ // 128) * p.D), F32)
    attn_outQ = [Buf(attn_out_h.ap()[:, qk * p.D:(qk + 1) * p.D])
                 for qk in range(p.QT)]
    for _aq in attn_outQ:
        link_pair(_aq, h_T)
        link_pair(_aq, down_acc)
    link_pair(h_T, down_acc)

    b_cols = max(p.DC * p.TQ, p.SUPER * p.D)
    arena_b = nc.alloc_sbuf_tensor("arena_b", [128, b_cols], BF16).ap()
    hq_T = Buf(arena_b)
    ternDown = Buf(arena_b)
    link(hq_T, ternDown)
    outheadsH = [Buf(arena_b[:, h * p.TQ:(h + 1) * p.TQ]) for h in range(p.H)]
    for _oh in outheadsH:
        link_pair(_oh, hq_T)
        link_pair(_oh, ternDown)

    arena_c = nc.alloc_sbuf_tensor("arena_c", [128, p.H * p.TQ], BF16).ap()
    Qb = Buf(arena_c)
    h2_T = Buf(arena_c)
    link(Qb, h2_T)

    o1, o2, o3 = p.NG * p.T, p.NG * p.T + p.TT * gw, p.NG * p.T + p.TT * gw + p.DC * gw
    d_attn = o3 + 2 * p.TQ + p.TT * 128
    d_ffn = p.SUPER * p.TQ + 4 * p.D
    arena_d_h = nc.alloc_sbuf_tensor("arena_d", [128, max(d_attn, d_ffn, 8192)], BF16)
    arena_d = arena_d_h.ap()
    Kg = Buf(arena_d[:, 0:o1])
    Vg = Buf(arena_d[:, o1:o2])
    wvg = Buf(arena_d[:, o2:o3])
    bc_sb = alias_at(arena_d_h, "bc_sb", (128, p.TQ), F32, delta=2 * o3)
    den_f = alias_at(arena_d_h, "den_f", (1, p.TQ), F32, delta=2 * o3)
    masks_b = Buf(arena_d[:, o3 + 2 * p.TQ:o3 + 2 * p.TQ + p.TT * 128])
    hidden = Buf(arena_d[:, 0:p.SUPER * p.TQ])
    tUpA = Buf(arena_d[:, p.SUPER * p.TQ + 0 * p.D:p.SUPER * p.TQ + 1 * p.D])
    tUpB = Buf(arena_d[:, p.SUPER * p.TQ + 1 * p.D:p.SUPER * p.TQ + 2 * p.D])
    tGtA = Buf(arena_d[:, p.SUPER * p.TQ + 2 * p.D:p.SUPER * p.TQ + 3 * p.D])
    tGtB = Buf(arena_d[:, p.SUPER * p.TQ + 3 * p.D:p.SUPER * p.TQ + 4 * p.D])
    link_pair(bc_sb, den_f)
    _s0 = p.SUPER * p.TQ
    _attn_rng = [(Kg, 0, o1), (Vg, o1, o2), (wvg, o2, o3),
                 (bc_sb, o3, o3 + 2 * p.TQ), (den_f, o3, o3 + 2 * p.TQ),
                 (masks_b, o3 + 2 * p.TQ, o3 + 2 * p.TQ + p.TT * 128)]
    _ffn_rng = [(hidden, 0, _s0),
                (tUpA, _s0, _s0 + p.D), (tUpB, _s0 + p.D, _s0 + 2 * p.D),
                (tGtA, _s0 + 2 * p.D, _s0 + 3 * p.D),
                (tGtB, _s0 + 3 * p.D, _s0 + 4 * p.D)]
    for _a, _a0, _a1 in _attn_rng:
        for _f, _f0, _f1 in _ffn_rng:
            if _a0 < _f1 and _f0 < _a1:
                link_pair(_a, _f)

    xf_cols = 3 * p.D
    arena_x_h = nc.alloc_sbuf_tensor("arena_x", [128, xf_cols], F32)
    arena_x = arena_x_h.ap()
    xtA = Buf(arena_x[:, 0:p.D])
    xtB = Buf(arena_x[:, p.D:2 * p.D])
    fsUpA = Buf(arena_x[:, 0:p.D // 2])
    fsUpB = Buf(arena_x[:, p.D // 2:p.D])
    fsGtA = Buf(arena_x[:, p.D:p.D + p.D // 2])
    fsGtB = Buf(arena_x[:, p.D + p.D // 2:2 * p.D])
    fsDnA = Buf(arena_x[:, 2 * p.D:2 * p.D + p.D // 2])
    fsDnB = Buf(arena_x[:, 2 * p.D + p.D // 2:3 * p.D])
    wqkA = alias_at(arena_x_h, "wqkA", (128, p.D // 2), BF16, delta=8 * p.D)
    wqkB = alias_at(arena_x_h, "wqkB", (128, p.D // 2), BF16, delta=9 * p.D)
    hbufA = alias_at(arena_x_h, "hbufA", (128, p.D), BF16, delta=8 * p.D)
    hbufB = alias_at(arena_x_h, "hbufB", (128, p.D), BF16, delta=10 * p.D)
    link_pair(xtA, fsUpA)
    link_pair(xtA, fsUpB)
    link_pair(xtB, fsGtA)
    link_pair(xtB, fsGtB)
    wqkC = alias_at(arena_x_h, "wqkC", (128, p.D // 2), BF16, delta=10 * p.D)
    wqkD = alias_at(arena_x_h, "wqkD", (128, p.D // 2), BF16, delta=11 * p.D)
    xtC = alias_at(arena_x_h, "xtC", (128, p.D), F32, delta=8 * p.D)
    for _b in (fsDnA, fsDnB, hbufA, hbufB, wqkA, wqkB, wqkC, wqkD):
        link_pair(xtC, _b)
    link_pair(hbufA, wqkA)
    link_pair(hbufA, wqkB)
    link_pair(hbufA, fsDnA)
    link_pair(wqkA, fsDnA)
    link_pair(wqkB, fsDnA)
    link_pair(hbufB, fsDnB)
    link_pair(hbufB, wqkC)
    link_pair(hbufB, wqkD)
    link_pair(fsDnB, wqkC)
    link_pair(fsDnB, wqkD)

    def sb(name, shape, dt=F32):
        return Buf(nc.alloc_sbuf_tensor(name, list(shape), dt).ap())

    stripA = sb("stripA", (128, max(p.QT * 128, p.D // 2)), BF16)
    stripB = sb("stripB", (128, max(p.QT * 128, p.D // 2)), BF16)
    qmA, qmB = stripA, stripB
    _sw = max(p.QT * 128, p.D // 2)
    stQ = [Buf((stripA if i < 2 else stripB).ap[:, (i % 2) * (_sw // 2):
                                                 (i % 2 + 1) * (_sw // 2)])
           for i in range(4)]
    for i, sq in enumerate(stQ):
        link_pair(sq, stripA if i < 2 else stripB)
    silu_t = sb("silu_t", (128, p.TQ), BF16)
    den_fA = sb("den_fA", (1, p.TQ))
    den_fB = sb("den_fB", (1, p.TQ))
    counts_b = sb("counts_b", (1, p.TQ), BF16)
    G_b_h = nc.alloc_sbuf_tensor("G_b", [p.TT, p.TQ], BF16)
    G_b = Buf(G_b_h.ap())
    qmC = alias_at(G_b_h, "qmC", (128, min(p.TQ, p.D // 2)), BF16)
    link(G_b, qmC)
    U_b = sb("U_b", (128, p.TT * p.TT), BF16)
    ident_b = sb("ident_b", (128, 128), F32)
    ones128 = sb("ones128", (128, 128), BF16)
    onesrow = sb("onesrow", (1, 128), F32)
    onescol = sb("onescol", (128, 1), F32)
    vsufT = sb("vsufT", (p.TT, gw), BF16)
    absP = sb("absP", (128, 3 * 72), F32)
    stats = sb("stats", (128, 6 * max(1, p.D // 512)), F32)
    mvA = sb("mvA", (128, 2), F32)
    mvC = sb("mvC", (128, 2), F32)
    nmrC = sb("nmrC", (128, 1), F32)
    rstdC = sb("rstdC", (128, 1), F32)
    mvB = sb("mvB", (128, 2), F32)
    nmrA = sb("nmrA", (128, 1), F32)
    nmrB = sb("nmrB", (128, 1), F32)
    rstdA = sb("rstdA", (128, 1), F32)
    rstdB = sb("rstdB", (128, 1), F32)
    sc_misc = sb("sc_misc", (1, 8), F32)
    s_up = sb("s_up", (128, 8), F32)
    s_gate = sb("s_gate", (128, 8), F32)
    s_down = sb("s_down", (128, 8), F32)
    epsd = sb("epsd", (128, 1), F32)
    epsb = sb("epsb", (128, 1), F32)
    epsp = sb("epsp", (128, 1), F32)
    issq = sb("issq", (128, 1), F32)

    R01 = nc.alloc_psum_tensor("R01", [128, 2048], F32).ap()
    R23 = nc.alloc_psum_tensor("R23", [128, 2048], F32).ap()
    psSA = Buf(R01[:, 0:1024])
    psSB = Buf(R01[:, 1024:2048])
    psOH = Buf(R23[:, 0:1024])
    psDen = Buf(R23[:, 1024:2048])
    psTr = Buf(R01[:, 0:512])
    psTr2 = Buf(R23[:, 0:512])
    downA = Buf(R01)
    downB = Buf(R23)
    psSA.others = [psTr, downA]
    psSB.others = [downA]
    psTr.others = [psSA, downA]
    downA.others = [psSA, psSB, psTr]
    psOH.others = [psTr2, downB]
    psDen.others = [downB]
    psTr2.others = [psOH, downB]
    downB.others = [psOH, psDen, psTr2]
    qkv8 = [Buf(R01[:, i * 512:(i + 1) * 512]) for i in range(4)] + \
           [Buf(R23[:, i * 512:(i + 1) * 512]) for i in range(4)]
    for i, hb in enumerate(qkv8):
        if i < 4:
            over = [psSA if i < 2 else psSB, downA] + ([psTr] if i == 0 else [])
        else:
            over = [psOH if i < 6 else psDen, downB] + ([psTr2] if i == 4 else [])
        hb.others = over
        for o in over:
            o.others = o.others + [hb]
    sSq = [Buf(R01[:, i * 512:(i + 1) * 512]) for i in range(4)]
    for i, sq in enumerate(sSq):
        over = [psSA if i < 2 else psSB, downA] + ([psTr] if i == 0 else [])
        over += [qkv8[i]]
        for o in over:
            link_pair(sq, o)
    qkv_rr = [0]

    def next_qreg():
        r = qkv8[qkv_rr[0] % 8]
        qkv_rr[0] += 1
        return r

    eP = Counter(nc, "eP")
    eV = Counter(nc, "eV")
    eA = Counter(nc, "eA")
    eO = Counter(nc, "eO")
    dma_ctrs = {}

    def qof(buf, tag):
        key = (id(buf), tag)
        if key not in dma_ctrs:
            dma_ctrs[key] = Counter(nc, f"q{tag}.{len(dma_ctrs)}", 16)
        return dma_ctrs[key]

    W = {k: Waiter() for k in ("sync", "gpsimd", "vector", "scalar", "tensor")}

    class _FakeIns:
        __slots__ = ("incs",)
        def __init__(self, incs):
            self.incs = incs
        def then_inc(self, sem, step):
            self.incs.append((sem, step))
            return self

    class Rec:
        def __init__(self):
            self.calls = []
        def wait_ge(self, sem, val):
            self.calls.append(("w", sem, val))
        def __getattr__(self, m):
            def f(*a, **k):
                incs = []
                self.calls.append(("c", m, a, k, incs))
                return _FakeIns(incs)
            return f

    recs = {k: Rec() for k in W}
    for k in W:
        W[k].eng = recs[k]

    with nc.Block() as blk:

        def sec(engname):
            def runner(fn):
                fn(recs[engname], W[engname])
            return runner

        def dma_in(engname, dst, dst_ap, src_ap, deps=(), tag="i"):
            q = qof(dst, tag)
            @sec(engname)
            def _(e, w):
                w.wait(*deps)
                dst.wr(w)
                tok = q.inc(e.dma_start(out=dst_ap, in_=src_ap))
                dst.note_write(tok)

        def dma_store(engname, dst_ap, src, src_ap, tag="o"):
            q = qof(src, tag)
            @sec(engname)
            def _(e, w):
                src.rd(w)
                tok = q.inc(e.dma_start(out=dst_ap, in_=src_ap))
                src.note_read(tok)

        dma_in("gpsimd", masks_b,
               masks_b.ap.rearrange("p (c n) -> p c n", c=p.TT),
               masks_in.rearrange("c p n -> p c n"))
        dma_in("gpsimd", G_b, G_b.ap[:, :], G_in[:, :])
        dma_in("gpsimd", counts_b, counts_b.ap[:, :], counts_in[:, :])
        dma_in("gpsimd", U_b,
               U_b.ap.rearrange("p (c n) -> p c n", c=p.TT),
               U_in.rearrange("c p n -> p c n"))
        dma_in("gpsimd", ident_b, ident_b.ap[:, :], ident_in[:, :])

        @sec("vector")
        def _(v, w):
            epsp.wr(w)
            D24 = float(1.5 * 2.0 ** -24)
            t = eV.inc(v.memset(epsp.ap[:, :], 1.0))
            for add in (D24, -1.0, -D24):
                w.wait(t)
                t = eV.inc(v.tensor_single_scalar(epsp.ap[:, :], epsp.ap[:, :],
                                                  add, ALU.add))
            epsp.note_write(t)
            for buf, val in ((ones128, 1.0), (onesrow, 1.0), (onescol, 1.0),
                             (epsb, 1e-5), (issq, float(p.T))):
                buf.wr(w)
                buf.note_write(eV.inc(v.memset(buf.ap[:, :], val)))
            w.wait(epsb.w[eV.sem.num])
            t = eV.inc(v.tensor_single_scalar(epsb.ap[:, :], epsb.ap[:, :],
                                              2.526212491824791e-13, ALU.add))
            epsb.note_write(t)

        @sec("scalar")
        def _(sc, w):
            issq.rd(w)
            t1 = eA.inc(sc.activation(issq.ap[:, :], issq.ap[:, :], AF.Ln))
            w.wait(t1)
            t2 = eA.inc(sc.activation(issq.ap[:, :], issq.ap[:, :], AF.Exp,
                                      scale=-0.5))
            issq.note_write(t2)

        def emit_ln_stats(src_buf, src_ap_fn, mv, rstd, eps_bias):
            ngp = max(1, p.D // 512)
            wdt = min(512, p.D)

            @sec("vector")
            def _(v, w):
                src_buf.rd(w)
                stats.wr(w)
                tk = None
                for gx in range(ngp):
                    tk = eV.inc(v.bn_stats(stats.ap[:, gx * 6:gx * 6 + 6],
                                           src_ap_fn()[:, gx * wdt:(gx + 1) * wdt]))
                stats.note_write(tk)
                src_buf.note_read(tk)
                w.wait(tk)
                mv.wr(w)
                mv.note_write(eV.inc(v.bn_aggr(mv.ap[:, :], stats.ap[:, 0:6 * ngp])))
                stats.note_read(Tok(eV.sem, eV.cnt))

            @sec("scalar")
            def _(s, w):
                mv.rd(w)
                rstd.wr(w)
                if isinstance(eps_bias, Buf):
                    eps_bias.rd(w)
                    bias = eps_bias.ap[:, 0:1]
                else:
                    bias = eps_bias
                t1 = eA.inc(s.activation(rstd.ap[:, :], mv.ap[:, 1:2], AF.Ln, bias=bias))
                mv.note_read(t1)
                if isinstance(eps_bias, Buf):
                    eps_bias.note_read(t1)
                w.wait(t1)
                rstd.note_write(eA.inc(
                    s.activation(rstd.ap[:, :], rstd.ap[:, :], AF.Exp, scale=-0.5)))

        def emit_normalize(src_buf, src_ap_fn, mv, rstd, dst_buf, dst_ap):
            nmr = nmrA if rstd is rstdA else (nmrB if rstd is rstdB else nmrC)

            @sec("vector")
            def _(v, w):
                mv.rd(w)
                rstd.rd(w)
                nmr.wr(w)
                t0 = eV.inc(v.tensor_scalar(nmr.ap[:, :], mv.ap[:, 0:1],
                                            -1.0, rstd.ap[:, 0:1],
                                            ALU.mult, ALU.mult))
                nmr.note_write(t0)
                mv.note_read(t0)
                rstd.note_read(t0)

            @sec("scalar")
            def _(sc, w):
                src_buf.rd(w)
                rstd.rd(w)
                nmr.rd(w)
                dst_buf.wr(w)
                tk = eA.inc(sc.activation(dst_ap, src_ap_fn(), AF.Identity,
                                          bias=nmr.ap[:, 0:1],
                                          scale=rstd.ap[:, 0:1]))
                dst_buf.note_write(tk)
                src_buf.note_read(tk)
                rstd.note_read(tk)
                nmr.note_read(tk)

        def load_xtile(xt, src_ap):
            hD = p.D // 2
            dma_in("sync", xt, xt.ap[:, 0:hD], src_ap[:, 0:hD])
            dma_in("gpsimd", xt, xt.ap[:, hD:p.D], src_ap[:, hD:p.D], tag="i2")

        def emit_transposes(hbuf, dstT, dst_width, t0, evac="scalar"):
            for grp in range((p.DC + 3) // 4):
                nj = min(4, p.DC - grp * 4)
                trb = psTr if grp % 2 == 0 else psTr2

                @sec("tensor")
                def _(t, w, grp=grp, nj=nj, trb=trb):
                    hbuf.rd(w)
                    ident_b.rd(w)
                    trb.wr(w)
                    ins = None
                    for j in range(nj):
                        dc = grp * 4 + j
                        ins = t.transpose(trb.ap[:, j * 128:(j + 1) * 128],
                                          hbuf.ap[:, dc * 128:(dc + 1) * 128],
                                          ident_b.ap[:, :])
                    tk = eP.inc(ins)
                    trb.note_write(tk)
                    hbuf.note_read(tk)
                    ident_b.note_read(tk)

                if evac == "mixed" and grp % 2 == 0:
                    @sec("vector")
                    def _(v, w, grp=grp, nj=nj, trb=trb):
                        trb.rd(w)
                        dstT.wr(w)
                        dst = dstT.ap.rearrange("p (c t) -> p c t", c=p.DC)
                        tk = eV.inc(v.tensor_copy(
                            dst[:, grp * 4:grp * 4 + nj, t0:t0 + 128],
                            trb.ap.rearrange("p (j t) -> p j t", j=4)[:, 0:nj, :]))
                        trb.note_read(tk)
                        dstT.note_write(tk)
                else:
                    @sec("scalar")
                    def _(sc, w, grp=grp, nj=nj, trb=trb):
                        trb.rd(w)
                        dstT.wr(w)
                        dst = dstT.ap.rearrange("p (c t) -> p c t", c=p.DC)
                        tk = eA.inc(sc.activation(
                            dst[:, grp * 4:grp * 4 + nj, t0:t0 + 128],
                            trb.ap.rearrange("p (j t) -> p j t", j=4)[:, 0:nj, :],
                            AF.Identity))
                        trb.note_read(tk)
                        dstT.note_write(tk)

        xrot = [(xtA, mvA, rstdA), (xtB, mvB, rstdB), (xtC, mvC, rstdC)]

        def emit_ln1():
            tiles = [("xs", t) for t in range(p.TT)] + [("xq", k) for k in range(p.QT)]
            n = len(tiles)

            def ld(i):
                srcname, ti = tiles[i]
                xt = xrot[i % 3][0]
                src = xs if srcname == "xs" else xq
                load_xtile(xt, src[ti * 128:(ti + 1) * 128, :])

            def st(i):
                xt, mv, rstd = xrot[i % 3]
                emit_ln_stats(xt, lambda xt=xt: xt.ap, mv, rstd, epsb)

            ld(0)
            st(0)
            if n > 1:
                ld(1)
            for i, (srcname, ti) in enumerate(tiles):
                xt, mv, rstd = xrot[i % 3]
                dstT = h_T if srcname == "xs" else hq_T
                dstW = p.T if srcname == "xs" else p.TQ
                t0 = ti * 128
                emit_normalize(xt, lambda xt=xt: xt.ap, mv, rstd, xt, xt.ap[:, :])
                if i + 1 < n:
                    st(i + 1)
                if i + 2 < n:
                    ld(i + 2)
                emit_transposes(xt, dstT, dstW, t0, evac="mixed")

        emit_ln1()

        abs_state = {"next": 0, "next_act": 0, "next_dve": 0}
        abs_sched_act = [(2, ec) for ec in range(p.EC)]
        abs_sched_dve = [(wi, ec) for ec in range(p.EC) for wi in (0, 1)]
        abs_sched = abs_sched_act + abs_sched_dve

        def emit_abs(n, which="any"):
            for _ in range(n):
                if which in ("act", "any") and \
                        abs_state["next_act"] < len(abs_sched_act):
                    wi, ec = abs_sched_act[abs_state["next_act"]]
                    abs_state["next_act"] += 1
                elif abs_state["next_dve"] < len(abs_sched_dve):
                    wi, ec = abs_sched_dve[abs_state["next_dve"]]
                    abs_state["next_dve"] += 1
                elif abs_state["next_act"] < len(abs_sched_act):
                    wi, ec = abs_sched_act[abs_state["next_act"]]
                    abs_state["next_act"] += 1
                else:
                    return
                i = abs_state["next"]
                abs_state["next"] += 1
                src = (UpL, GateL, DownR)[wi]
                buf = xtA if i % 2 == 0 else xtB
                col = wi * 72 + ec
                dma_in("sync", buf, buf.ap[:, :], src[ec])
                if wi < 2:
                    @sec("vector")
                    def _(v2, w, buf=buf, col=col):
                        buf.rd(w)
                        tk = eV.inc(v2.reduce_sum(absP.ap[:, col:col + 1],
                                                  buf.ap[:, :], axis=AX.X,
                                                  apply_absolute_value=True))
                        buf.note_read(tk)
                        absP.note_write(tk)
                else:
                    @sec("scalar")
                    def _(s2, w, buf=buf, col=col):
                        buf.rd(w)
                        tk = eA.inc(s2.activation(buf.ap[:, :], buf.ap[:, :], AF.Abs,
                                                  accum_out=absP.ap[:, col:col + 1]))
                        buf.note_read(tk)
                        buf.note_write(tk)
                        absP.note_write(tk)

        wpairs = [(wqkA, wqkB), (wqkC, wqkD)]

        def load_w_halves(dram, nb, pair, deps=()):
            dma_in("gpsimd", pair[0], pair[0].ap[:, :], dram[nb][:, 0:p.D // 2], deps)
            dma_in("gpsimd", pair[1], pair[1].ap[:, :], dram[nb][:, p.D // 2:p.D], deps)

        def emit_proj_mms(nb, dstT_buf, dst_off, rhsT_buf, rhs_w, pair,
                          scale_buf=None):
            wsA, wsB = pair
            nqb = (rhs_w + 511) // 512
            for tb in range(nqb):
                wdt = min(512, rhs_w - tb * 512)
                reg = next_qreg()

                @sec("tensor")
                def _(t, w, tb=tb, reg=reg, wdt=wdt):
                    wsA.rd(w)
                    wsB.rd(w)
                    rhsT_buf.rd(w)
                    reg.wr(w)
                    rT = rhsT_buf.ap.rearrange("p (c t) -> p c t", c=p.DC)
                    ins = None
                    for dc in range(p.DC):
                        wsl = (wsA if dc < p.DC // 2 else wsB)
                        wof = (dc if dc < p.DC // 2 else dc - p.DC // 2) * 128
                        ins = t.matmul(reg.ap[:, 0:wdt],
                                       wsl.ap[:, wof:wof + 128],
                                       rT[:, dc, tb * 512:tb * 512 + wdt],
                                       start=(dc == 0), stop=(dc == p.DC - 1))
                    tk = eP.inc(ins)
                    reg.note_write(tk)
                    wsA.note_read(tk)
                    wsB.note_read(tk)
                    rhsT_buf.note_read(tk)

                @sec("vector")
                def _(v, w, tb=tb, reg=reg, wdt=wdt):
                    reg.rd(w)
                    dstT_buf.wr(w)
                    dst = dstT_buf.ap[:, dst_off + tb * 512:dst_off + tb * 512 + wdt]
                    if scale_buf is None:
                        tk = eV.inc(v.tensor_copy(dst, reg.ap[:, 0:wdt]))
                    else:
                        scale_buf.rd(w)
                        tk = eV.inc(v.tensor_single_scalar(
                            dst, reg.ap[:, 0:wdt], scale_buf.ap[:, 0:1], ALU.mult))
                        scale_buf.note_read(tk)
                    reg.note_read(tk)
                    dstT_buf.note_write(tk)

        def kpair(ki):
            return wpairs[(p.DC + ki) % 2]

        nK = p.NGRP * p.NG
        load_w_halves(WqL, 0, wpairs[0])
        if p.DC > 1:
            load_w_halves(WqL, 1, wpairs[1])
        for nb in range(p.DC):
            emit_proj_mms(nb, Qb, nb * p.TQ, hq_T, p.TQ, wpairs[nb % 2],
                          scale_buf=issq)
            if nb + 2 < p.DC:
                load_w_halves(WqL, nb + 2, wpairs[nb % 2])
            elif nb + 2 - p.DC < min(2, nK):
                load_w_halves(WkL, nb + 2 - p.DC, kpair(nb + 2 - p.DC))
            emit_abs(2)

        def load_wvg(gi):
            h0 = gi * p.NG
            qwv = qof(wvg, "i")

            @sec("gpsimd")
            def _(g, w):
                wvg.wr(w)
                tk = None
                for dc in range(p.DC):
                    tk = qwv.inc(g.dma_start(
                        out=wvg.ap[:, dc * gw:(dc + 1) * gw],
                        in_=WvR[dc][:, h0 * 128:h0 * 128 + gw]))
                wvg.note_write(tk)

        def emit_group(gi):
            h0 = gi * p.NG
            for hh in range(p.NG):
                emit_proj_mms(h0 + hh, Kg, hh * p.T, h_T, p.T,
                              kpair(gi * p.NG + hh))
                emit_abs(2)
            for hh2 in range(p.NG):
                ki2 = (gi + 1) * p.NG + hh2
                if ki2 < nK:
                    load_w_halves(WkL, ki2, kpair(ki2))

            if gi == 0:
                load_wvg(0)

            for st in range(p.TT):
                reg = next_qreg()

                @sec("tensor")
                def _(t, w, st=st, reg=reg):
                    wvg.rd(w)
                    h_T.rd(w)
                    reg.wr(w)
                    hT = h_T.ap.rearrange("p (c t) -> p c t", c=p.DC)
                    ins = None
                    for dc in range(p.DC):
                        ins = t.matmul(reg.ap[:, 0:gw],
                                       hT[:, dc, st * 128:(st + 1) * 128],
                                       wvg.ap[:, dc * gw:(dc + 1) * gw],
                                       start=(dc == 0), stop=(dc == p.DC - 1))
                    tk = eP.inc(ins)
                    reg.note_write(tk)
                    wvg.note_read(tk)
                    h_T.note_read(tk)

                @sec("vector")
                def _(v, w, st=st, reg=reg):
                    reg.rd(w)
                    Vg.wr(w)
                    tk = eV.inc(v.tensor_copy(Vg.ap[:, st * gw:(st + 1) * gw],
                                              reg.ap[:, 0:gw]))
                    reg.note_read(tk)
                    Vg.note_write(tk)
                if st % 4 == 3:
                    emit_abs(1, which="dve")

            @sec("tensor")
            def _(t, w):
                U_b.rd(w)
                Vg.rd(w)
                psDen.wr(w)
                ins = None
                for c in range(p.TT):
                    ins = t.matmul(psDen.ap[0:p.TT, 0:gw],
                                   U_b.ap[:, c * p.TT:(c + 1) * p.TT],
                                   Vg.ap[:, c * gw:(c + 1) * gw],
                                   start=(c == 0), stop=(c == p.TT - 1),
                                   skip_group_check=True)
                tk = eP.inc(ins)
                psDen.note_write(tk)
                U_b.note_read(tk)
                Vg.note_read(tk)

            @sec("vector")
            def _(v, w):
                psDen.rd(w)
                vsufT.wr(w)
                tk = eV.inc(v.tensor_copy(vsufT.ap[:, :], psDen.ap[0:p.TT, 0:gw]))
                psDen.note_read(tk)
                vsufT.note_write(tk)

            if gi + 1 < p.NGRP:
                load_wvg(gi + 1)

            for hh in range(p.NG):
                emit_head(gi, hh)

        quarter_pool = p.TQ >= 512

        def emit_head(gi, hh):
            habs = gi * p.NG + hh
            chunks = [c for c in range(p.TT) if p.n_c[c] > 0]

            alloc = []
            ptr = 0
            for c in chunks:
                cols = p.n_c[c] * 128
                if quarter_pool:
                    ns = 1 if cols <= 512 else 2
                    if ns == 2 and ptr % 2 == 1:
                        ptr += 1
                    base = ptr % 4
                    ptr += ns
                    sbufs = sSq[base:base + ns]
                    s_ap = R01[:, base * 512:base * 512 + cols]
                    st_src = stripA if base < 2 else stripB
                    st0 = (base % 2) * 512
                    tbufs = stQ[base:base + ns]
                    t_ap = st_src.ap[:, st0:st0 + cols]
                else:
                    base = ptr % 2
                    ptr += 1
                    sbufs = [psSA if base == 0 else psSB]
                    s_ap = R01[:, base * 1024:base * 1024 + cols]
                    tb = stripA if base == 0 else stripB
                    tbufs = [tb]
                    t_ap = tb.ap[:, 0:cols]
                alloc.append((cols, sbufs, s_ap, tbufs, t_ap))

            def emit_S(ci):
                c = chunks[ci]
                cols, sbufs, s_ap, _, _ = alloc[ci]

                @sec("tensor")
                def _(t, w, c=c, cols=cols, sbufs=sbufs, s_ap=s_ap):
                    Kg.rd(w)
                    Qb.rd(w)
                    for b in sbufs:
                        b.wr(w)
                    ins = None
                    for qb in range((cols + 511) // 512):
                        wdt = min(512, cols - qb * 512)
                        ins = t.matmul(
                            s_ap[:, qb * 512:qb * 512 + wdt],
                            Kg.ap[:, hh * p.T + c * 128:hh * p.T + (c + 1) * 128],
                            Qb.ap[:, habs * p.TQ + qb * 512:habs * p.TQ + qb * 512 + wdt],
                            start=True, stop=True)
                    tk = eP.inc(ins)
                    for b in sbufs:
                        b.note_write(tk)
                    Kg.note_read(tk)
                    Qb.note_read(tk)

            emit_S(0)
            if len(chunks) > 1:
                emit_S(1)
            for ci, c in enumerate(chunks):
                cols, sbufs, s_ap, tbufs, t_ap = alloc[ci]

                @sec("vector")
                def _(v, w, c=c, cols=cols, sbufs=sbufs, s_ap=s_ap):
                    for b in sbufs:
                        b.rd(w)
                    masks_b.rd(w)
                    tk = eV.inc(v.tensor_tensor(s_ap[:, cols - 128:cols],
                                                s_ap[:, cols - 128:cols],
                                                masks_b.ap[:, c * 128:(c + 1) * 128],
                                                ALU.mult))
                    sbufs[-1].note_write(tk)
                    masks_b.note_read(tk)

                @sec("scalar")
                def _(s, w, cols=cols, sbufs=sbufs, s_ap=s_ap, tbufs=tbufs,
                      t_ap=t_ap):
                    for b in sbufs:
                        b.rd(w)
                    for b in tbufs:
                        b.wr(w)
                    tk = eA.inc(s.activation(t_ap, s_ap[:, 0:cols], AF.Exp))
                    for b in sbufs:
                        b.note_read(tk)
                    for b in tbufs:
                        b.note_write(tk)

                if ci + 2 < len(chunks):
                    emit_S(ci + 2)
                if ci % 2 == 0:
                    emit_abs(1, which="dve")

                @sec("tensor")
                def _(t, w, ci=ci, c=c, cols=cols, tbufs=tbufs, t_ap=t_ap):
                    for b in tbufs:
                        b.rd(w)
                    ones128.rd(w)
                    Vg.rd(w)
                    if ci == 0:
                        psDen.wr(w)
                        psOH.wr(w)
                    first = (ci == 0)
                    ins = None
                    for qb in range((cols + 511) // 512):
                        wdt = min(512, cols - qb * 512)
                        ins = t.matmul(psDen.ap[:, qb * 512:qb * 512 + wdt],
                                       ones128.ap[:, :],
                                       t_ap[:, qb * 512:qb * 512 + wdt],
                                       start=first, stop=False,
                                       skip_group_check=True)
                    tden = eP.inc(ins)
                    for qb in range((cols + 511) // 512):
                        wdt = min(512, cols - qb * 512)
                        ins = t.matmul(
                            psOH.ap[:, qb * 512:qb * 512 + wdt],
                            Vg.ap[:, c * gw + hh * 128:c * gw + (hh + 1) * 128],
                            t_ap[:, qb * 512:qb * 512 + wdt],
                            start=first, stop=False, skip_group_check=True)
                    tpv = eP.inc(ins)
                    psDen.note_write(tden)
                    psOH.note_write(tpv)
                    for b in tbufs:
                        b.note_read(tpv)
                    ones128.note_read(tden)
                    Vg.note_read(tpv)

            @sec("tensor")
            def _(t, w):
                vsufT.rd(w)
                G_b.rd(w)
                ins = None
                for qb in range(max(1, p.TQ // 512)):
                    wdt = min(512, p.TQ)
                    ins = t.matmul(psOH.ap[:, qb * 512:qb * 512 + wdt],
                                   vsufT.ap[0:p.TT, hh * 128:(hh + 1) * 128],
                                   G_b.ap[0:p.TT, qb * 512:qb * 512 + wdt],
                                   start=False, stop=True, skip_group_check=True)
                tk = eP.inc(ins)
                psOH.note_write(tk)
                vsufT.note_read(tk)
                G_b.note_read(tk)

            @sec("vector")
            def _(v, w):
                psDen.rd(w)
                counts_b.rd(w)
                den_f.wr(w)
                tk = eV.inc(v.tensor_add(den_f.ap[:, :], psDen.ap[0:1, 0:p.TQ],
                                         counts_b.ap[:, :]))
                den_f.note_write(tk)
                psDen.note_read(tk)
                counts_b.note_read(tk)

            @sec("scalar")
            def _(s, w):
                den_f.rd(w)
                t1 = eA.inc(s.activation(den_f.ap[:, :], den_f.ap[:, :], AF.Ln))
                den_f.note_write(t1)
                w.wait(t1)
                t2 = eA.inc(s.activation(den_f.ap[:, :], den_f.ap[:, :],
                                         AF.Exp, scale=-1.0))
                den_f.note_write(t2)

            @sec("gpsimd")
            def _(e, w, habs=habs):
                den_f.rd(w)
                tok = qof(den_f, "o").inc(e.dma_start(
                    out=den_spill[habs:habs + 1, :], in_=den_f.ap[:, :]))
                den_f.note_read(tok)
                den_store_toks[habs] = tok

            @sec("vector")
            def _(v, w):
                psOH.rd(w)
                outheadsH[habs].wr(w)
                tk = eV.inc(v.tensor_copy(
                    outheadsH[habs].ap[:, :],
                    psOH.ap[:, 0:p.TQ]))
                outheadsH[habs].note_write(tk)
                psOH.note_read(tk)

        den_store_toks = {}
        for gi in range(p.NGRP):
            emit_group(gi)

        for habs in range(p.H):
            breg = psDen if habs % 2 == 0 else psOH
            dfx = den_fA if habs % 2 == 0 else den_fB
            dma_in("sync", dfx, dfx.ap[:, :], den_spill[habs:habs + 1, :],
                   deps=(den_store_toks[habs],))

            @sec("tensor")
            def _(t, w, habs=habs, breg=breg, dfx=dfx):
                dfx.rd(w)
                onesrow.rd(w)
                breg.wr(w)
                ins = None
                for qb in range(max(1, p.TQ // 512)):
                    wdt = min(512, p.TQ)
                    ins = t.matmul(breg.ap[:, qb * 512:qb * 512 + wdt],
                                   onesrow.ap[0:1, :],
                                   dfx.ap[0:1, qb * 512:qb * 512 + wdt],
                                   start=True, stop=True, skip_group_check=True)
                tk = eP.inc(ins)
                breg.note_write(tk)
                onesrow.note_read(tk)
                dfx.note_read(tk)

            @sec("vector")
            def _(v, w, habs=habs, breg=breg):
                breg.rd(w)
                outheadsH[habs].wr(w)
                oh = outheadsH[habs].ap[:, :]
                tk = eV.inc(v.tensor_tensor(oh, oh, breg.ap[:, 0:p.TQ],
                                            ALU.mult))
                outheadsH[habs].note_write(tk)
                breg.note_read(tk)

        if debug:
            for _h in range(p.H):
                dma_store("gpsimd", dbg_oh[:, _h * p.TQ:(_h + 1) * p.TQ],
                          outheadsH[_h], outheadsH[_h].ap[:, :])
            dma_store("gpsimd", dbg_q, Qb, Qb.ap[:, :])
            dma_store("gpsimd", dbg_k, Kg, Kg.ap[:, :])
            dma_store("gpsimd", dbg_v, Vg, Vg.ap[:, :])

        emit_abs(len(abs_sched))

        SIGMA = float(2.0 ** 19)

        @sec("vector")
        def _(v, w):
            absP.rd(w)
            bc_sb.wr(w)
            q_ap = bc_sb.ap[:, 0:p.EC]
            r_ap = bc_sb.ap[:, 64:64 + p.EC]
            tk = None
            for wi in range(3):
                cols = absP.ap[:, wi * 72:wi * 72 + p.EC]
                t1 = eV.inc(v.tensor_scalar(q_ap, cols, SIGMA, SIGMA,
                                            ALU.add, ALU.subtract))
                w.wait(t1)
                t2 = eV.inc(v.tensor_tensor(r_ap, cols, q_ap, ALU.subtract))
                w.wait(t2)
                t3 = eV.inc(v.reduce_sum(absP.ap[:, wi * 72 + 70:wi * 72 + 71],
                                         q_ap, axis=AX.X))
                tk = eV.inc(v.reduce_sum(absP.ap[:, wi * 72 + 71:wi * 72 + 72],
                                         r_ap, axis=AX.X))
                w.wait(tk)
            absP.note_write(tk)
            bc_sb.note_write(tk)

        @sec("tensor")
        def _(t, w):
            absP.rd(w)
            onescol.rd(w)
            psSA.wr(w)
            ap6 = absP.ap.rearrange("p (w c) -> p w c", w=3)[:, :, 70:72]
            tk = eP.inc(t.matmul(psSA.ap[0:1, 0:6], onescol.ap[:, :], ap6,
                                 start=True, stop=True, skip_group_check=True))
            psSA.note_write(tk)
            absP.note_read(tk)
            onescol.note_read(tk)

        @sec("vector")
        def _(v, w):
            psSA.rd(w)
            sc_misc.wr(w)
            epsp.rd(w)
            t0 = eV.inc(v.tensor_copy(sc_misc.ap[:, 2:8], psSA.ap[0:1, 0:6]))
            w.wait(t0)
            qs = sc_misc.ap[0:1, 2:8].rearrange("p (w two) -> p w two", two=2)
            t1 = eV.inc(v.tensor_tensor(sc_misc.ap[:, 0:3], qs[:, :, 0],
                                        qs[:, :, 1], ALU.add))
            w.wait(t1)
            t2 = eV.inc(v.tensor_single_scalar(sc_misc.ap[:, 0:3],
                                               sc_misc.ap[:, 0:3],
                                               1.0 / (p.E * p.D), ALU.mult))
            w.wait(t2)
            t3 = eV.inc(v.tensor_single_scalar(sc_misc.ap[:, 3:6],
                                               sc_misc.ap[:, 0:3],
                                               epsp.ap[0:1, 0:1], ALU.mult))
            w.wait(t3)
            t4 = eV.inc(v.scalar_tensor_tensor(sc_misc.ap[:, 0:3],
                                               sc_misc.ap[:, 3:6], -12.5,
                                               sc_misc.ap[:, 0:3],
                                               ALU.mult, ALU.add))
            w.wait(t4)
            tk = eV.inc(v.tensor_single_scalar(sc_misc.ap[:, 0:3],
                                               sc_misc.ap[:, 0:3],
                                               p.QEPS, ALU.add))
            sc_misc.note_write(tk)
            psSA.note_read(tk)
            epsp.note_read(tk)

        @sec("tensor")
        def _(t, w):
            sc_misc.rd(w)
            onesrow.rd(w)
            psSB.wr(w)
            tk = eP.inc(t.matmul(psSB.ap[:, 0:3], onesrow.ap[0:1, :],
                                 sc_misc.ap[0:1, 0:3], start=True, stop=True,
                                 skip_group_check=True))
            psSB.note_write(tk)
            sc_misc.note_read(tk)
            onesrow.note_read(tk)

        @sec("vector")
        def _(v, w):
            psSB.rd(w)
            epsp.rd(w)
            for buf, col in ((s_up, 0), (s_gate, 1), (s_down, 2)):
                buf.wr(w)
                for dst, mul in ((0, -1.0), (1, 0.5), (2, -0.5)):
                    tk = eV.inc(v.tensor_single_scalar(
                        buf.ap[:, dst:dst + 1], psSB.ap[:, col:col + 1],
                        mul, ALU.mult))
                w.wait(tk)
                tk = eV.inc(v.tensor_single_scalar(
                    buf.ap[:, 3:4], psSB.ap[:, col:col + 1],
                    epsp.ap[:, 0:1], ALU.mult))
                w.wait(tk)
                tk = eV.inc(v.tensor_single_scalar(
                    buf.ap[:, 4:5], buf.ap[:, 3:4], -1.0, ALU.mult))
                w.wait(tk)
                t6 = eV.inc(v.tensor_tensor(buf.ap[:, 6:7], buf.ap[:, 1:2],
                                            buf.ap[:, 3:4], ALU.add))
                w.wait(t6)
                tk = eV.inc(v.tensor_single_scalar(buf.ap[:, 5:6],
                                                   buf.ap[:, 6:7], -1.0,
                                                   ALU.mult))
                buf.note_write(tk)
                psSB.note_read(tk)
            epsd.wr(w)
            s_down.rd(w)
            t1 = eV.inc(v.tensor_tensor(epsd.ap[:, :], s_down.ap[:, 0:1],
                                        s_down.ap[:, 0:1], ALU.mult))
            s_down.note_read(t1)
            w.wait(t1)
            t2 = eV.inc(v.reciprocal(epsd.ap[:, :], epsd.ap[:, :]))
            w.wait(t2)
            epsb.rd(w)
            t3 = eV.inc(v.tensor_tensor(epsd.ap[:, :], epsd.ap[:, :],
                                        epsb.ap[:, :], ALU.mult))
            epsb.note_read(t3)
            epsd.note_write(t3)

        if debug:
            @sec("gpsimd")
            def _(g, w):
                s_up.rd(w); s_gate.rd(w); s_down.rd(w); epsd.rd(w)
                tk = None
                for i, b in enumerate((s_up, s_gate, s_down)):
                    tk = qof(b, "dbg").inc(g.dma_start(out=dbg_scales[:, i*3:(i+1)*3], in_=b.ap[:, 0:3]))
                    b.note_read(tk)
                tk = qof(epsd, "dbg").inc(g.dma_start(out=dbg_scales[:, 9:10], in_=epsd.ap[:, :], allow_slow_non_contiguous=True))
                epsd.note_read(tk)


        def emit_wo():
            ndb = max(1, p.D // 512)
            wdt = min(512, p.D)
            for db in range(ndb):
                for dc in range(p.DC):
                    wslab = hbufA if dc % 2 == 0 else hbufB
                    dma_in("gpsimd", wslab, wslab.ap[:, 0:wdt],
                           WoR[dc][:, db * 512:db * 512 + wdt])

                    @sec("tensor")
                    def _(t, w, db=db, dc=dc, wslab=wslab):
                        wslab.rd(w)
                        outheadsH[dc].rd(w)
                        ins = None
                        for qk in range(p.QT):
                            reg = qkv8[qk % 8]
                            if dc == 0:
                                reg.wr(w)
                            ins = t.matmul(
                                reg.ap[:, 0:wdt],
                                outheadsH[dc].ap[:, qk * 128:(qk + 1) * 128],
                                wslab.ap[:, 0:wdt],
                                start=(dc == 0), stop=(dc == p.DC - 1),
                                skip_group_check=True)
                        tk = eP.inc(ins)
                        wslab.note_read(tk)
                        outheadsH[dc].note_read(tk)
                        for qk in range(p.QT):
                            qkv8[qk % 8].note_write(tk)

                for qk in range(p.QT):
                    reg = qkv8[qk % 8]

                    @sec("vector")
                    def _(v, w, qk=qk, reg=reg, db=db):
                        reg.rd(w)
                        attn_outQ[qk].wr(w)
                        tk = eV.inc(v.tensor_copy(
                            attn_outQ[qk].ap[:, db * 512:db * 512 + wdt],
                            reg.ap[:, 0:wdt]))
                        reg.note_read(tk)
                        attn_outQ[qk].note_write(tk)

        emit_wo()
        if debug:
            for _q in range(p.QT):
                dma_store("gpsimd", dbg_ao[:, _q * p.D:(_q + 1) * p.D],
                          attn_outQ[_q], attn_outQ[_q].ap[:, :])

        def emit_x1_ln2():
            n = p.QT

            def ld(qk):
                xt = xrot[qk % 3][0]
                load_xtile(xt, xq[qk * 128:(qk + 1) * 128, :])

            def st_attn(qk):
                _, mv, rstd = xrot[qk % 3]
                emit_ln_stats(attn_outQ[qk],
                              lambda qk=qk: attn_outQ[qk].ap[:, :],
                              mv, rstd, epsb)

            def norm_attn(qk):
                _, mv, rstd = xrot[qk % 3]
                emit_normalize(attn_outQ[qk],
                               lambda qk=qk: attn_outQ[qk].ap[:, :],
                               mv, rstd, attn_outQ[qk],
                               attn_outQ[qk].ap[:, :])

            ld(0)
            st_attn(0)
            norm_attn(0)
            if n > 1:
                ld(1)
                st_attn(1)
            for qk in range(n):
                xt, mv, rstd = xrot[qk % 3]
                if qk + 1 < n:
                    norm_attn(qk + 1)

                @sec("vector")
                def _(v, w, qk=qk, xt=xt):
                    xt.rd(w)
                    attn_outQ[qk].rd(w)
                    tk = eV.inc(v.tensor_add(xt.ap[:, :], xt.ap[:, :],
                                             attn_outQ[qk].ap[:, :]))
                    xt.note_write(tk)
                    attn_outQ[qk].note_read(tk)

                if qk + 2 < n:
                    st_attn(qk + 2)
                    ld(qk + 2)
                dma_store("sync", x1_spill[qk * 128:(qk + 1) * 128, 0:p.D // 2],
                          xt, xt.ap[:, 0:p.D // 2])
                dma_store("gpsimd", x1_spill[qk * 128:(qk + 1) * 128, p.D // 2:p.D],
                          xt, xt.ap[:, p.D // 2:p.D], tag="o2")
                emit_ln_stats(xt, lambda xt=xt: xt.ap, mv, rstd, epsb)
                emit_normalize(xt, lambda xt=xt: xt.ap, mv, rstd, xt, xt.ap[:, :])
                emit_transposes(xt, h2_T, p.TQ, qk * 128)

        emit_x1_ln2()
        if debug:
            dma_store("gpsimd", dbg_h2t, h2_T, h2_T.ap[:, :])

        def quant_sign(fslab, scales, qm, tern_buf, tern_off):
            dst = tern_buf.ap[:, tern_off:tern_off + p.D // 2]

            @sec("scalar")
            def _(sc, w):
                fslab.rd(w)
                scales.rd(w)
                tern_buf.wr(w)
                t1 = eA.inc(sc.activation(dst, fslab.ap[:, :],
                                          AF.Sign, bias=scales.ap[:, 5:6]))
                tern_buf.note_write(t1)
                qm.wr(w)
                t2 = eA.inc(sc.activation(qm.ap[:, 0:p.D // 2], fslab.ap[:, :],
                                          AF.Sign, bias=scales.ap[:, 6:7]))
                qm.note_write(t2)
                fslab.note_read(t2)
                scales.note_read(t2)

            @sec("vector")
            def _(v, w):
                qm.rd(w)
                tern_buf.rd(w)
                tern_buf.wr(w)
                tk = eV.inc(v.tensor_add(dst, dst, qm.ap[:, 0:p.D // 2]))
                tern_buf.note_write(tk)
                qm.note_read(tk)

        def quant3(fslab, scales, qm, tern_buf, tern_off, exact):
            dst = tern_buf.ap[:, tern_off:tern_off + p.D // 2]

            @sec("vector")
            def _(v, w):
                fslab.rd(w)
                scales.rd(w)
                tern_buf.wr(w)
                if exact:
                    t1 = eV.inc(v.tensor_scalar(dst, fslab.ap[:, :],
                                                scales.ap[:, 1:2],
                                                scales.ap[:, 4:5],
                                                ALU.add, ALU.is_lt))
                else:
                    t1 = eV.inc(v.tensor_single_scalar(dst, fslab.ap[:, :],
                                                       scales.ap[:, 5:6],
                                                       ALU.is_lt))
                tern_buf.note_write(t1)
                qm.wr(w)
                if exact:
                    t2 = eV.inc(v.tensor_scalar(qm.ap[:, 0:p.D // 2],
                                                fslab.ap[:, :],
                                                scales.ap[:, 1:2],
                                                scales.ap[:, 3:4],
                                                ALU.subtract, ALU.is_gt))
                else:
                    t2 = eV.inc(v.tensor_single_scalar(qm.ap[:, 0:p.D // 2],
                                                       fslab.ap[:, :],
                                                       scales.ap[:, 6:7],
                                                       ALU.is_gt))
                qm.note_write(t2)
                w.wait(t1, t2)
                tern_buf.wr(w)
                if exact:
                    tk = eV.inc(v.tensor_tensor(dst, qm.ap[:, 0:p.D // 2], dst,
                                                ALU.subtract))
                else:
                    tk = eV.inc(v.tensor_tensor(dst, dst, qm.ap[:, 0:p.D // 2],
                                                ALU.subtract))
                tern_buf.note_write(tk)
                qm.note_read(tk)
                fslab.note_read(t2)
                scales.note_read(t2)

        def ffn_quant_stage(ec, part="all"):
            ecl = ec % p.SUPER
            tu = tUpA if ec % 2 == 0 else tUpB
            tg = tGtA if ec % 2 == 0 else tGtB
            hD = p.D // 2
            if part in ("all", "pre"):
                for half, fs, qm in ((0, fsUpA, qmA), (1, fsUpB, qmC)):
                    dma_in("sync", fs, fs.ap[:, :],
                           UpL[ec][:, half * hD:(half + 1) * hD])
                    quant_sign(fs, s_up, qm, tu, half * hD)
                for half, fs, qm in ((0, fsGtA, qmA), (1, fsGtB, qmC)):
                    dma_in("gpsimd", fs, fs.ap[:, :],
                           GateL[ec][:, half * hD:(half + 1) * hD])
                    quant_sign(fs, s_gate, qm, tg, half * hD)
                if part == "pre":
                    for half, fs in ((0, fsDnA), (1, fsDnB)):
                        dma_in("gpsimd", fs, fs.ap[:, :],
                               DownR[ec][:, half * hD:(half + 1) * hD])
            if part in ("all", "post"):
                for half, fs in ((0, fsDnA), (1, fsDnB)):
                    if part == "all":
                        dma_in("gpsimd", fs, fs.ap[:, :],
                               DownR[ec][:, half * hD:(half + 1) * hD])
                    quant3(fs, s_down, qmB, ternDown, ecl * p.D + half * hD,
                           exact=True)

        def emit_ffn():
            nth = max(1, p.TQ // 512)
            wdt = min(512, p.TQ)
            ffn_quant_stage(0)
            for sup in range(p.NSUP):
                for ecl in range(p.SUPER):
                    ec = sup * p.SUPER + ecl
                    tu = tUpA if ec % 2 == 0 else tUpB
                    tg = tGtA if ec % 2 == 0 else tGtB
                    upreg = psSA if ec % 2 == 0 else psSB
                    gtreg = psOH if ec % 2 == 0 else psDen
                    if ec + 1 < p.EC and (ec + 1) % p.SUPER != 0:
                        ffn_quant_stage(ec + 1)

                    for th in range(nth):
                        @sec("tensor")
                        def _(t, w, tu=tu, tg=tg, th=th, upreg=upreg, gtreg=gtreg):
                            tu.rd(w)
                            tg.rd(w)
                            h2_T.rd(w)
                            if th == 0:
                                upreg.wr(w)
                                gtreg.wr(w)
                            h2 = h2_T.ap.rearrange("p (c t) -> p c t", c=p.DC)
                            ins = None
                            for dc in range(p.DC):
                                ins = t.matmul(upreg.ap[:, th * 512:th * 512 + wdt],
                                               tu.ap[:, dc * 128:(dc + 1) * 128],
                                               h2[:, dc, th * 512:th * 512 + wdt],
                                               start=(dc == 0), stop=(dc == p.DC - 1),
                                               skip_group_check=True)
                            tku = eP.inc(ins)
                            for dc in range(p.DC):
                                ins = t.matmul(gtreg.ap[:, th * 512:th * 512 + wdt],
                                               tg.ap[:, dc * 128:(dc + 1) * 128],
                                               h2[:, dc, th * 512:th * 512 + wdt],
                                               start=(dc == 0), stop=(dc == p.DC - 1),
                                               skip_group_check=True)
                            tkg = eP.inc(ins)
                            upreg.note_write(tku)
                            gtreg.note_write(tkg)
                            tu.note_read(tkg)
                            tg.note_read(tkg)
                            h2_T.note_read(tkg)

                    @sec("scalar")
                    def _(s, w, upreg=upreg):
                        upreg.rd(w)
                        s_up.rd(w)
                        silu_t.wr(w)
                        tk = eA.inc(s.activation(silu_t.ap[:, 0:p.TQ],
                                                 upreg.ap[:, 0:p.TQ],
                                                 AF.Silu, scale=s_up.ap[:, 1:2]))
                        silu_t.note_write(tk)
                        upreg.note_read(tk)
                        s_up.note_read(tk)

                    @sec("vector")
                    def _(v, w, ecl=ecl, gtreg=gtreg):
                        gtreg.rd(w)
                        silu_t.rd(w)
                        s_gate.rd(w)
                        hidden.wr(w)
                        tk = eV.inc(v.scalar_tensor_tensor(
                            hidden.ap[:, ecl * p.TQ:(ecl + 1) * p.TQ],
                            gtreg.ap[:, 0:p.TQ], s_gate.ap[:, 1:2],
                            silu_t.ap[:, 0:p.TQ], ALU.mult, ALU.mult))
                        hidden.note_write(tk)
                        gtreg.note_read(tk)
                        silu_t.note_read(tk)
                        s_gate.note_read(tk)

                if sup + 1 < p.NSUP:
                    ffn_quant_stage((sup + 1) * p.SUPER, part="pre")

                for tki in range(p.TQ // 128):
                    reg = downA if tki % 2 == 0 else downB
                    ndb = max(1, p.D // 512)
                    dwdt = min(512, p.D)

                    @sec("tensor")
                    def _(t, w, sup=sup, tki=tki, reg=reg, ndb=ndb, dwdt=dwdt):
                        hidden.rd(w)
                        ternDown.rd(w)
                        reg.wr(w)
                        ins = None
                        for ecl in range(p.SUPER):
                            for db in range(ndb):
                                ins = t.matmul(
                                    reg.ap[:, db * 512:db * 512 + dwdt],
                                    hidden.ap[:, ecl * p.TQ + tki * 128:
                                              ecl * p.TQ + (tki + 1) * 128],
                                    ternDown.ap[:, ecl * p.D + db * 512:
                                                ecl * p.D + db * 512 + dwdt],
                                    start=(ecl == 0), stop=(ecl == p.SUPER - 1),
                                    skip_group_check=True)
                        tk = eP.inc(ins)
                        reg.note_write(tk)
                        hidden.note_read(tk)
                        ternDown.note_read(tk)

                    @sec("vector")
                    def _(v, w, sup=sup, tki=tki, reg=reg):
                        reg.rd(w)
                        down_acc.wr(w)
                        dst = down_acc.ap[:, tki * p.D:(tki + 1) * p.D]
                        if sup == 0:
                            ins = v.tensor_copy(dst, reg.ap[:, 0:p.D])
                        else:
                            ins = v.tensor_add(dst, dst, reg.ap[:, 0:p.D])
                        tk = eV.inc(ins)
                        reg.note_read(tk)
                        down_acc.note_write(tk)

                if sup + 1 < p.NSUP:
                    ffn_quant_stage((sup + 1) * p.SUPER, part="post")

        emit_ffn()
        if debug:
            dma_store("gpsimd", dbg_down, down_acc, down_acc.ap[:, 0:(p.TQ // 128) * p.D])

        def emit_final():
            n = p.TQ // 128

            def ld(qk):
                xt = xrot[qk % 3][0]
                load_xtile(xt, x1_spill[qk * 128:(qk + 1) * 128, :])

            def st_down(qk):
                _, mv, rstd = xrot[qk % 3]
                emit_ln_stats(down_acc,
                              lambda qk=qk: down_acc.ap[:, qk * p.D:(qk + 1) * p.D],
                              mv, rstd, epsd)

            def norm_down(qk):
                _, mv, rstd = xrot[qk % 3]
                emit_normalize(down_acc,
                               lambda qk=qk: down_acc.ap[:, qk * p.D:(qk + 1) * p.D],
                               mv, rstd, down_acc,
                               down_acc.ap[:, qk * p.D:(qk + 1) * p.D])

            ld(0)
            st_down(0)
            norm_down(0)
            if n > 1:
                ld(1)
                st_down(1)
            for qk in range(n):
                xt, mv, rstd = xrot[qk % 3]

                @sec("vector")
                def _(v, w, qk=qk, xt=xt):
                    xt.rd(w)
                    down_acc.rd(w)
                    tk = eV.inc(v.tensor_add(xt.ap[:, :], xt.ap[:, :],
                                             down_acc.ap[:, qk * p.D:(qk + 1) * p.D]))
                    xt.note_write(tk)
                    down_acc.note_read(tk)

                if qk + 1 < n:
                    norm_down(qk + 1)
                if qk + 2 < n:
                    st_down(qk + 2)
                    ld(qk + 2)

                dma_store("sync", out_d[qk * 128:(qk + 1) * 128, 0:p.D // 2],
                          xt, xt.ap[:, 0:p.D // 2])
                dma_store("gpsimd", out_d[qk * 128:(qk + 1) * 128, p.D // 2:p.D],
                          xt, xt.ap[:, p.D // 2:p.D], tag="o2")

        emit_final()

        @sec("sync")
        def _(sy, w):
            for ctr in dma_ctrs.values():
                if ctr.cnt:
                    sy.wait_ge(ctr.sem, ctr.cnt)

        def _replay(rec):
            def cb(eng):
                for e in rec.calls:
                    if e[0] == "w":
                        eng.wait_ge(e[1], e[2])
                    else:
                        _, m, a, k, incs = e
                        ins = getattr(eng, m)(*a, **k)
                        for sem, step in incs:
                            ins.then_inc(sem, step)
            return cb

        for engname in ("sync", "gpsimd", "vector", "scalar", "tensor"):
            getattr(blk, engname)(_replay(recs[engname]))

    return nc


def make_in_maps(par, x, Wq, Wk, Wv, Wo, Wup, Wgate, Wdown):
    lay = host_layouts(par, Wq, Wk, Wv, Wo, Wup, Wgate, Wdown)
    WqL, WkL, WvR, WoR, UpL, GateL, DownR = lay
    consts = [core_consts(par, parity) for parity in (0, 1)]
    in_maps, perms = [], []
    for core in range(2 * par.B):
        b, parity = core // 2, core % 2
        own = par.owned[parity]
        perm = np.concatenate([np.arange(g * 128, (g + 1) * 128) for g in own])
        perms.append(perm)
        cc = consts[parity]
        in_maps.append(dict(
            xs=x[b], xq=np.ascontiguousarray(x[b][perm]),
            WqL=WqL, WkL=WkL, WvR=WvR, WoR=WoR,
            UpL=UpL, GateL=GateL, DownR=DownR,
            masks=cc["masks"], Gmat=cc["Gmat"], counts=cc["counts"],
            Umat=cc["Umat"], ident=cc["ident"]))
    return in_maps, perms


def kernel(L0_attn_x, L0_attn_Wq, L0_attn_Wk, L0_attn_Wv, L0_attn_Wo,
           L0_ffn_W_up, L0_ffn_W_gate, L0_ffn_W_down):
    from concourse.bass_utils import run_bass_kernel_spmd

    par = Par()
    x = np.ascontiguousarray(np.asarray(L0_attn_x, dtype=np.float32))
    in_maps, perms = make_in_maps(
        par, x,
        np.asarray(L0_attn_Wq, np.float32), np.asarray(L0_attn_Wk, np.float32),
        np.asarray(L0_attn_Wv, np.float32), np.asarray(L0_attn_Wo, np.float32),
        np.asarray(L0_ffn_W_up, np.float32), np.asarray(L0_ffn_W_gate, np.float32),
        np.asarray(L0_ffn_W_down, np.float32))
    nc = build_program(par)
    res = run_bass_kernel_spmd(nc, in_maps, core_ids=list(range(8)))
    shard0 = np.asarray(res.results[0]["out"])
    out = np.empty((par.B, par.T, par.D), shard0.dtype)
    for core in range(2 * par.B):
        b = core // 2
        out[b][perms[core]] = np.asarray(res.results[core]["out"])
    return out
